# version 12
# baseline (speedup 1.0000x reference)
import numpy as np
import ml_dtypes
from contextlib import ExitStack
import concourse.bass as bass
import concourse.mybir as mybir
from concourse.bass_utils import run_bass_kernel_spmd

BF = ml_dtypes.bfloat16
F32, BF16 = mybir.dt.float32, mybir.dt.bfloat16
AF = mybir.ActivationFunctionType
ALU = mybir.AluOpType
AX = mybir.AxisListType.X

NH, HD, BLKSZ = 32, 128, 256
TOK = 1024
TC = 8
KC = 32
NJ = 4
DM = 4096


def _decays():
    base = np.float64(2.0) ** np.float64(-0.25)
    exponent = np.arange(NH, dtype=np.float64) + 1.0
    factor = np.float64(1.0) - np.float64(1.0) / (31 + 1e-5) + np.float64(1e-5)
    slope = base ** exponent * factor
    r = np.arange(BLKSZ, dtype=np.float64) + 1.0
    qd = np.exp(-slope[:, None] * r[None, :])
    kd = np.exp(-slope[:, None] * (BLKSZ - r)[None, :])
    diff = r[:, None] - r[None, :]
    dd = np.where(diff >= 0, np.exp(-slope[:, None, None] * diff[None]), 0.0)
    bd = np.exp(-slope * BLKSZ)
    return slope, qd, kd, dd, bd


class _H:
    __slots__ = ("env", "ins")
    def __init__(self, env, ins):
        self.env, self.ins = env, ins
    def inc(self, semname, k):
        self.env.cnt[semname] = self.env.cnt.get(semname, 0) + k
        if self.ins is not None:
            self.ins.then_inc(self.env.sems[semname], k)
        return self


class _P:
    def __init__(self, env, real):
        self._env, self._real = env, real
    def __getattr__(self, name):
        env, real = self._env, self._real
        def call(*a, **k):
            ins = getattr(real, name)(*a, **k) if real is not None else None
            return _H(env, ins)
        return call
    def wait(self, semname, val=None):
        v = self._env.cnt.get(semname, 0) if val is None else val
        if v <= 0:
            return
        if self._real is not None:
            self._real.wait_ge(self._env.sems[semname], v)


class _Env:
    pass


def build_nc():
    nc = bass.Bass("TRN2", target_bir_lowering=False)

    x_h = nc.dram_tensor("x", [TOK, DM], F32, kind="ExternalInput")
    wqk_h = nc.dram_tensor("wqk", [64, 128, KC, 128], F32, kind="ExternalInput")
    wv_h = nc.dram_tensor("wv", [DM, DM], F32, kind="ExternalInput")
    wg_h = nc.dram_tensor("wg", [DM, DM], F32, kind="ExternalInput")
    wo_h = nc.dram_tensor("wo", [DM, DM], F32, kind="ExternalInput")
    wb_h = nc.dram_tensor("wb", [DM, DM], F32, kind="ExternalInput")
    wr_h = nc.dram_tensor("wr", [128, 256], F32, kind="ExternalInput")
    wa_h = nc.dram_tensor("wa", [DM, 128], F32, kind="ExternalInput")
    wbf_h = nc.dram_tensor("wbf", [128, DM], F32, kind="ExternalInput")
    nw_h = nc.dram_tensor("nw", [DM], F32, kind="ExternalInput")
    bb_h = nc.dram_tensor("bbv", [DM], F32, kind="ExternalInput")
    ddt_h = nc.dram_tensor("ddt", [NH, 2, 128, 256], F32, kind="ExternalInput")
    kdt_h = nc.dram_tensor("kdt", [128, 64], F32, kind="ExternalInput")
    qdt_h = nc.dram_tensor("qdt", [128, 64], F32, kind="ExternalInput")
    cw_h = nc.dram_tensor("cw", [NH, 4], F32, kind="ExternalInput")
    idm_h = nc.dram_tensor("idm", [128, 128], F32, kind="ExternalInput")
    out_h = nc.dram_tensor("out", [TOK, DM], F32, kind="ExternalOutput")
    qkscr = nc.dram_tensor("qkscr", [64, 128, 1024], BF16)
    gscr = nc.dram_tensor("gscr", [TOK, DM], BF16)
    cl_h = [nc.dram_tensor(f"cl{h}", [128, 128], F32) for h in range(NH)]
    ca_h = [nc.dram_tensor(f"ca{h}", [4, 128, 128], F32) for h in range(NH)]

    _, _, _, _, bd64 = _decays()
    bdf = [float(np.float32(v)) for v in bd64]

    with ExitStack() as ctx:
        sb = lambda n, s, d: ctx.enter_context(nc.sbuf_tensor(n, s, d))
        psf = lambda n: ctx.enter_context(nc.psum_tensor(n, [128, 512], F32))
        R1 = sb("R1", [128, 32768], BF16)
        R2 = sb("R2", [128, 32768], BF16)
        xbuf = sb("xbuf", [128, 2048], F32)
        stage = sb("stage", [128, 4096], F32)
        wst = sb("wst", [128, 4096], BF16)
        evq = sb("evq", [128, 2048], BF16)
        nwb = sb("nwb", [128, 4096], BF16)
        qh2 = sb("qh2", [128, 2048], BF16)
        ddb = sb("ddb", [128, 512], F32)
        smb = sb("smb", [128, 2048], BF16)
        ckt = sb("ckt", [128, 1024], BF16)
        cvp = sb("cvp", [128, 1024], BF16)
        cbuf = sb("cbuf", [128, 1024], BF16)
        sfall = sb("sfall", [128, 512], F32)
        sfa = sb("sfa", [128, 128], F32)
        sba = sb("sba", [128, 1024], BF16)
        lca = sb("lca", [128, 128], F32)
        lcb = sb("lcb", [128, 128], F32)
        kdc = sb("kdc", [128, 64], F32)
        qdc = sb("qdc", [128, 64], F32)
        cwb = sb("cwb", [128, 128], F32)
        crd = sb("crd", [128, 1024], F32)
        cmbt = sb("cmbt", [128, 64], F32)
        hwf = sb("hwf", [128, 256], F32)
        hwT = sb("hwT", [128, 1024], BF16)
        idb = sb("idb", [128, 128], F32)
        stt_ = sb("stats", [128, 16], F32)
        rwk = sb("rwk", [128, 288], F32)
        rv1 = sb("rv1", [128, 8], F32)
        bst = sb("bst", [128, 1024], BF16)
        wrb = sb("wrb", [128, 256], BF16)
        pf = [psf(f"pf{i}") for i in range(8)]

        semnames = ["sx", "sw", "sp", "sd", "ss", "sm", "smh", "smb_", "sv",
                    "sc", "scc", "so", "st_"]
        sems = {n: ctx.enter_context(nc.semaphore(n)) for n in semnames}
        blk = ctx.enter_context(nc.Block())

        def R1xn(kc, half):
            return R1[:, kc * 1024 + half * 512: kc * 1024 + (half + 1) * 512]

        def R1t(kc, tc):
            return R1[:, kc * 1024 + tc * 128: kc * 1024 + (tc + 1) * 128]

        def emit_all(env):
            t, v, s, g, y = env.t, env.v, env.s, env.g, env.y
            C = env.cnt

            def barrier(gx=None):
                for e in (t, v, s, g, y):
                    for sn in semnames:
                        if sn == "scc" or C.get(sn, 0) <= 0:
                            continue
                        if e is g and gx is not None and sn not in gx:
                            continue
                        e.wait(sn)

            g.dma_start(idb[:], idm_h.ap()).inc("sp", 16)
            g.dma_start(nwb[:], bass.AP(nw_h, 0, [[0, 128], [1, 4096]])).inc("sp", 16)
            g.dma_start(kdc[:], kdt_h.ap()).inc("sp", 16)
            g.dma_start(qdc[:], qdt_h.ap()).inc("sp", 16)
            g.dma_start(cwb[:], bass.AP(cw_h, 0, [[0, 128], [1, 128]])).inc("sp", 16)
            g.dma_start(wrb[:], wr_h.ap()).inc("sp", 16)
            for e in (t, v, s):
                e.wait("sp")

            stage_free = 0
            for tc in range(TC):
                half_sx = [0, 0]
                for half in range(2):
                    if half == 0:
                        y.wait("sv", stage_free)
                    y.dma_start(stage[:, half * 2048:(half + 1) * 2048],
                                x_h.ap()[tc * 128:(tc + 1) * 128,
                                         half * 2048:(half + 1) * 2048]).inc("sx", 16)
                    half_sx[half] = C["sx"]
                    v.wait("sx", half_sx[half])
                    v.reduce_sum(stt_[:, 12 + half:13 + half],
                                 stage[:, half * 2048:(half + 1) * 2048], axis=AX).inc("sv", 1)
                    s.wait("sx", half_sx[half])
                    s.activation(R2[:, 0:2048], stage[:, half * 2048:(half + 1) * 2048],
                                 AF.Square, accum_out=stt_[:, 14 + half:15 + half]).inc("sc", 1)
                v.wait("sv")
                v.tensor_tensor(stt_[:, 0:1], stt_[:, 12:13], stt_[:, 13:14], ALU.add).inc("sv", 1)
                v.wait("sv")
                v.tensor_scalar(stt_[:, 1:2], stt_[:, 0:1], -1.0 / DM, None, ALU.mult).inc("sv", 1)
                v.wait("sv")
                v.tensor_mul(stt_[:, 2:3], stt_[:, 1:2], stt_[:, 1:2]).inc("sv", 1)
                v.wait("sc")
                v.wait("sv")
                v.tensor_tensor(stt_[:, 3:4], stt_[:, 14:15], stt_[:, 15:16], ALU.add).inc("sv", 1)
                v.wait("sv")
                v.scalar_tensor_tensor(stt_[:, 4:5], stt_[:, 3:4], 1.0 / DM, stt_[:, 2:3],
                                       ALU.mult, ALU.subtract).inc("sv", 1)
                v.wait("sv")
                v.tensor_scalar(stt_[:, 4:5], stt_[:, 4:5], 1e-5, None, ALU.add).inc("sv", 1)
                v.wait("sv")
                v.reciprocal(stt_[:, 5:6], stt_[:, 4:5]).inc("sv", 1)
                s.wait("sv")
                s.activation(stt_[:, 6:7], stt_[:, 5:6], AF.Sqrt).inc("sc", 1)
                for half in range(2):
                    v.wait("sc")
                    if tc > 0 or half > 0:
                        v.wait("sm")
                    v.tensor_scalar(xbuf[:], stage[:, half * 2048:(half + 1) * 2048],
                                    stt_[:, 1:2], stt_[:, 6:7],
                                    ALU.add, ALU.mult).inc("sv", 1)
                    xnh = C["sv"]
                    if half == 1:
                        stage_free = xnh
                    for kcl in range(16):
                        kc = half * 16 + kcl
                        if kcl >= 8:
                            t.wait("sv", xnh + kcl - 7)
                        else:
                            t.wait("sv", xnh)
                        t.transpose(pf[kc % 8][:, 0:128], xbuf[:, kcl * 128:(kcl + 1) * 128],
                                    idb[:]).inc("sm", 1)
                        v.wait("sm")
                        v.tensor_copy(R1[:, kc * 1024 + tc * 128: kc * 1024 + (tc + 1) * 128],
                                      pf[kc % 8][:, 0:128]).inc("sv", 1)
            barrier(gx=[])

            def aform_real(w_hh, evac_cb):
                base_smh = C.get("smh", 0)
                base_sw = C.get("sw", 0)
                base_sc = C.get("sc", 0)
                gi = 0
                for fo in range(8):
                    for kg in range(16):
                        if gi >= 4:
                            g.wait("smh", base_smh + gi - 3)
                        g.dma_start(
                            wst[:, (gi % 4) * 1024:(gi % 4) * 1024 + 1024],
                            bass.AP(w_hh, (2 * kg * 128) * DM + fo * 512,
                                    [[DM, 128], [128 * DM, 2], [1, 512]])).inc("sw", 16)
                        t.wait("sw", base_sw + 16 * (gi + 1))
                        for j in range(2):
                            kc = 2 * kg + j
                            for tc in range(TC):
                                if kc == 0 and fo > 0:
                                    t.wait("sc", base_sc + (fo - 1) * 8 + tc + 1)
                                h_ = t.matmul(pf[tc][:], R1t(kc, tc),
                                              wst[:, (gi % 4) * 1024 + j * 512:
                                                  (gi % 4) * 1024 + (j + 1) * 512],
                                              start=(kc == 0), stop=(kc == 31))
                                if tc == TC - 1 and j == 1:
                                    h_.inc("smh", 1)
                        gi += 1
                    for tc in range(TC):
                        evac_cb(fo, tc)

            p1_smh = C.get("smh", 0)
            def v_evac(fo, tc):
                s.wait("smh", p1_smh + (fo + 1) * 16)
                s.activation(R2[:, tc * 4096 + fo * 512: tc * 4096 + (fo + 1) * 512],
                             pf[tc][:], AF.Silu).inc("sc", 1)
            aform_real(wv_h, v_evac)
            barrier(gx=["smh"])

            p2_smh0 = C.get("smh", 0)
            p2_sw0 = C.get("sw", 0)
            ST = {"hi": 0}
            Z = -1
            dd_cnt = [0] * NH; kspill = [0] * NH; scores_cnt = [0] * NH
            mask_cnt = [0] * NH; cmm = [0] * NH; intra_ev = [0] * NH
            horner = [0] * NH; carry_ss = [0] * NH; crd_cnt = [0] * NH
            qld = [0] * NH; cast_cnt = [0] * NH; inter_mm = [0] * NH
            schain = [0] * NH; qevac = [0] * NH; kevac = [0] * NH
            ck_cnt = [0] * NH; cvp_cnt = [0] * NH

            def qk_chunk(h, piece):
                c = 2 * h + piece
                pi = 2 * piece
                for half in range(2):
                    hi = ST["hi"]
                    if hi >= 2:
                        g.wait("smh", p2_smh0 + hi - 1)
                    g.dma_start(
                        wst[:, (hi % 2) * 2048:(hi % 2) * 2048 + 2048],
                        bass.AP(wqk_h, c * 128 * DM + half * 2048,
                                [[DM, 128], [1, 2048]])).inc("sw", 16)
                    t.wait("sw", p2_sw0 + 16 * (hi + 1))
                    if half == 0 and h > 0:
                        t.wait("sc", (qevac if piece == 0 else kevac)[h - 1])
                    for kcl in range(16):
                        kc = half * 16 + kcl
                        lhsT = wst[:, (hi % 2) * 2048 + kcl * 128:
                                   (hi % 2) * 2048 + (kcl + 1) * 128]
                        t.matmul(pf[pi][:], lhsT, R1xn(kc, 0),
                                 start=(kc == 0), stop=(kc == 31))
                        h1 = t.matmul(pf[pi + 1][:], lhsT, R1xn(kc, 1),
                                      start=(kc == 0), stop=(kc == 31))
                        if kcl == 15:
                            h1.inc("smh", 1)
                    ST["hi"] += 1
                s.wait("smh")
                for bhalf in range(2):
                    s.activation(evq[:, piece * 1024 + bhalf * 512:
                                     piece * 1024 + (bhalf + 1) * 512],
                                 pf[pi + bhalf][:], AF.Silu).inc("sc", 1)
                if piece == 0:
                    qevac[h] = C["sc"]
                else:
                    kevac[h] = C["sc"]
                y.wait("sc")
                y.dma_start(qkscr.ap()[c, :, :],
                            evq[:, piece * 1024:(piece + 1) * 1024]).inc("ss", 16)
                if piece == 1:
                    kspill[h] = C["ss"]

            for hh in range(NH + 3):
                h1_ = hh - 1
                h2_ = hh - 2
                h3_ = hh - 3
                if 0 <= h1_ < NH:
                    v.wait("sm", scores_cnt[h1_])
                    v.wait("sd", dd_cnt[h1_])
                    for j in range(NJ):
                        for mc in range(2):
                            v.tensor_mul(smb[:, j * 512 + mc * 256: j * 512 + (mc + 1) * 256],
                                         pf[4 + j][:, mc * 256:(mc + 1) * 256],
                                         ddb[:, mc * 256:(mc + 1) * 256]).inc("sv", 1)
                    mask_cnt[h1_] = C["sv"]
                if hh < NH:
                    if h1_ >= 0:
                        y.wait("sv", mask_cnt[h1_])
                    y.dma_start(ddb[:], bass.AP(ddt_h, hh * 65536,
                                                [[256, 128], [32768, 2], [1, 256]])).inc("sd", 16)
                    dd_cnt[hh] = C["sd"]
                if 0 <= h2_ < NH:
                    g.wait("ss", carry_ss[h2_])
                    g.collective_compute("AllGather", ALU.bypass,
                                         replica_groups=[[0, 1, 2, 3], [4, 5, 6, 7]],
                                         ins=[cl_h[h2_].ap()],
                                         outs=[ca_h[h2_].ap()]).inc("scc", 1)
                    if h2_ >= 2:
                        y.wait("sm", inter_mm[h2_ - 2])
                    y.dma_start(qh2[:, (h2_ % 2) * 1024:(h2_ % 2 + 1) * 1024],
                                qkscr.ap()[2 * h2_, :, :]).inc("sx", 16)
                    qld[h2_] = C["sx"]
                if 0 <= h3_ < NH:
                    s.wait("scc", h3_ + 1)
                    if h3_ >= 2:
                        s.wait("sv", schain[h3_ - 2])
                    s.dma_start(crd[:, (h3_ % 2) * 512:(h3_ % 2) * 512 + 512],
                                bass.AP(ca_h[h3_], 0,
                                        [[128, 128], [16384, 4], [1, 128]])).inc("st_", 16)
                    crd_cnt[h3_] = C["st_"]
                if 0 <= h1_ < NH:
                    s.wait("ss", kspill[h1_])
                    if h1_ >= 1:
                        s.wait("sm", cmm[h1_ - 1])
                    for j in range(NJ):
                        for mc in range(2):
                            s.dma_start_transpose(
                                ckt[:, j * 256 + mc * 128: j * 256 + (mc + 1) * 128],
                                qkscr.ap()[2 * h1_ + 1, :,
                                           j * 256 + mc * 128: j * 256 + (mc + 1) * 128]
                            ).inc("st_", 16)
                    ck_cnt[h1_] = C["st_"]
                if 0 <= h3_ < NH:
                    p3_ = h3_ % 2
                    v.wait("st_", crd_cnt[h3_])
                    v.tensor_scalar(sfa[:], crd[:, p3_ * 512: p3_ * 512 + 128],
                                    cwb[:, h3_ * 4: h3_ * 4 + 1], None, ALU.mult).inc("sv", 1)
                    v.wait("sv")
                    v.scalar_tensor_tensor(sfall[:, 0:128],
                                           crd[:, p3_ * 512 + 128: p3_ * 512 + 256],
                                           cwb[:, h3_ * 4 + 1: h3_ * 4 + 2], sfa[:],
                                           ALU.mult, ALU.add).inc("sv", 1)
                    v.wait("sv")
                    v.scalar_tensor_tensor(sfa[:],
                                           crd[:, p3_ * 512 + 256: p3_ * 512 + 384],
                                           cwb[:, h3_ * 4 + 2: h3_ * 4 + 3],
                                           sfall[:, 0:128], ALU.mult, ALU.add).inc("sv", 1)
                    v.wait("sv")
                    v.scalar_tensor_tensor(sfall[:, 0:128],
                                           crd[:, p3_ * 512 + 384: p3_ * 512 + 512],
                                           cwb[:, h3_ * 4 + 3: h3_ * 4 + 4], sfa[:],
                                           ALU.mult, ALU.add).inc("sv", 1)
                    for j in range(1, NJ):
                        v.wait("sv")
                        v.scalar_tensor_tensor(sfall[:, j * 128:(j + 1) * 128],
                                               sfall[:, (j - 1) * 128: j * 128], bdf[h3_],
                                               cbuf[:, p3_ * 512 + (j - 1) * 128:
                                                    p3_ * 512 + j * 128],
                                               ALU.mult, ALU.add).inc("sv", 1)
                    schain[h3_] = C["sv"]
                    s.wait("sv", schain[h3_])
                    if h3_ >= 2:
                        s.wait("sm", inter_mm[h3_ - 2])
                    s.activation(sba[:, p3_ * 512:(p3_ + 1) * 512], sfall[:],
                                 AF.Copy).inc("sc", 1)
                    cast_cnt[h3_] = C["sc"]
                if hh < NH:
                    qk_chunk(hh, 0)
                if 0 <= h1_ < NH:
                    t.wait("sv", mask_cnt[h1_])
                    t.wait("st_", ck_cnt[h1_])
                    for j in range(NJ):
                        bank = pf[4 + j]
                        for nc_ in range(2):
                            for mc in range(2):
                                t.matmul(bank[:, nc_ * 128:(nc_ + 1) * 128],
                                         smb[:, j * 512 + mc * 256 + nc_ * 128:
                                             j * 512 + mc * 256 + (nc_ + 1) * 128],
                                         R2[:, (2 * j + mc) * 4096 + h1_ * 128:
                                            (2 * j + mc) * 4096 + (h1_ + 1) * 128],
                                         start=(mc == 0), stop=(mc == 1))
                        for mc in range(2):
                            h_ = t.matmul(bank[:, 256:384],
                                          ckt[:, j * 256 + mc * 128: j * 256 + (mc + 1) * 128],
                                          cvp[:, j * 256 + mc * 128: j * 256 + (mc + 1) * 128],
                                          start=(mc == 0), stop=(mc == 1))
                        h_.inc("sm", 1)
                    cmm[h1_] = C["sm"]
                    p1_ = h1_ % 2
                    for j in range(NJ):
                        bank = pf[4 + j]
                        v.wait("sm", cmm[h1_] - (NJ - 1 - j))
                        for nc_ in range(2):
                            v.tensor_copy(R2[:, (2 * j + nc_) * 4096 + h1_ * 128:
                                             (2 * j + nc_) * 4096 + (h1_ + 1) * 128],
                                          bank[:, nc_ * 128:(nc_ + 1) * 128]).inc("sv", 1)
                        v.tensor_copy(cbuf[:, p1_ * 512 + j * 128: p1_ * 512 + (j + 1) * 128],
                                      bank[:, 256:384]).inc("sv", 1)
                        if j == 0:
                            v.tensor_copy(lca[:], bank[:, 256:384]).inc("sv", 1)
                        else:
                            lsrc, ldst = (lca, lcb) if j % 2 == 1 else (lcb, lca)
                            if j == 1 and h1_ >= 1:
                                v.wait("ss", carry_ss[h1_ - 1])
                            v.wait("sv")
                            v.scalar_tensor_tensor(ldst[:], lsrc[:], bdf[h1_],
                                                   bank[:, 256:384],
                                                   ALU.mult, ALU.add).inc("sv", 1)
                    intra_ev[h1_] = C["sv"]
                    horner[h1_] = C["sv"]
                    y.wait("sv", horner[h1_])
                    y.dma_start(cl_h[h1_].ap(), lcb[:]).inc("ss", 16)
                    carry_ss[h1_] = C["ss"]
                if 0 <= h3_ < NH:
                    p3_ = h3_ % 2
                    t.wait("sc", cast_cnt[h3_])
                    t.wait("sx", qld[h3_])
                    if 0 <= h1_ < NH:
                        t.wait("sv", intra_ev[h1_])
                    for j in range(NJ):
                        bank = pf[4 + j]
                        for nc_ in range(2):
                            h_ = t.matmul(bank[:, nc_ * 128:(nc_ + 1) * 128],
                                          qh2[:, p3_ * 1024 + j * 256 + nc_ * 128:
                                              p3_ * 1024 + j * 256 + (nc_ + 1) * 128],
                                          sba[:, p3_ * 512 + j * 128: p3_ * 512 + (j + 1) * 128],
                                          start=True, stop=True)
                        h_.inc("sm", 1)
                    inter_mm[h3_] = C["sm"]
                    for j in range(NJ):
                        bank = pf[4 + j]
                        v.wait("sm", inter_mm[h3_] - (NJ - 1 - j))
                        for nc_ in range(2):
                            sl = R2[:, (2 * j + nc_) * 4096 + h3_ * 128:
                                    (2 * j + nc_) * 4096 + (h3_ + 1) * 128]
                            v.scalar_tensor_tensor(sl, bank[:, nc_ * 128:(nc_ + 1) * 128],
                                                   qdc[:, h3_ * 2 + nc_: h3_ * 2 + nc_ + 1],
                                                   sl, ALU.mult, ALU.add).inc("sv", 1)
                if hh < NH:
                    qk_chunk(hh, 1)
                    if h1_ >= 0:
                        v.wait("sm", cmm[h1_])
                    for j in range(NJ):
                        for mc in range(2):
                            v.tensor_scalar(cvp[:, j * 256 + mc * 128: j * 256 + (mc + 1) * 128],
                                            R2[:, (2 * j + mc) * 4096 + hh * 128:
                                               (2 * j + mc) * 4096 + (hh + 1) * 128],
                                            kdc[:, 2 * hh + mc: 2 * hh + mc + 1], None,
                                            ALU.mult).inc("sv", 1)
                    cvp_cnt[hh] = C["sv"]
                    t.wait("sc", kevac[hh])
                    t.wait("sv")
                    for j in range(NJ):
                        for mc in range(2):
                            h_ = t.matmul(pf[4 + j][:, mc * 256:(mc + 1) * 256],
                                          evq[:, 1024 + j * 256 + mc * 128:
                                              1024 + j * 256 + (mc + 1) * 128],
                                          evq[:, j * 256:(j + 1) * 256],
                                          start=True, stop=True)
                    h_.inc("sm", 1)
                    scores_cnt[hh] = C["sm"]
            barrier(gx=["smh"])

            p3_smh = C.get("smh", 0)
            p3_ss = C.get("ss", 0)
            def g_evac(fo, tc):
                slot = tc % 4
                idx = fo * 8 + tc
                if idx >= 4:
                    s.wait("ss", p3_ss + 16 * (idx - 4 + 1))
                s.wait("smh", p3_smh + (fo + 1) * 16)
                s.activation(evq[:, slot * 512:(slot + 1) * 512], pf[tc][:],
                             AF.Sigmoid).inc("sc", 1)
                y.wait("sc")
                y.dma_start(gscr.ap()[tc * 128:(tc + 1) * 128, fo * 512:(fo + 1) * 512],
                            evq[:, slot * 512:(slot + 1) * 512]).inc("ss", 16)
            aform_real(wg_h, g_evac)
            barrier(gx=["smh"])

            g.dma_start(wst[:, 0:4096],
                        bass.AP(wa_h, 0, [[128, 128], [16384, 32], [1, 128]])).inc("sw", 16)
            wa_cnt = C["sw"]
            g.wait("smh")
            g.dma_start(out_h.ap(), x_h.ap()).inc("ss", 16)
            xcopy = C["ss"]
            for kc in range(KC):
                for half in range(2):
                    h_ = t.matmul(pf[6 + half][0:8, :], wrb[:, kc * 8:(kc + 1) * 8],
                                  R1xn(kc, half), start=(kc == 0), stop=(kc == 31))
                    if kc == 31:
                        h_.inc("sm", 1)
            v.wait("sm")
            for half in range(2):
                v.tensor_copy(stage[0:8, half * 512:(half + 1) * 512],
                              pf[6 + half][0:8, :]).inc("sv", 1)
            lgev = C["sv"]
            t.wait("sv", lgev)
            for tc in range(TC):
                if tc >= 2:
                    t.wait("sv")
                h_ = t.transpose(pf[tc % 2][:, 0:8],
                                 stage[0:8, tc * 128:(tc + 1) * 128],
                                 idb[0:8, 0:8]).inc("sm", 1)
                v.wait("sm")
                v.tensor_copy(rwk[:, tc * 8:(tc + 1) * 8], pf[tc % 2][:, 0:8]).inc("sv", 1)
            lgcopy = C["sv"]
            def rep8(col0):
                return bass.AP(rwk, col0, [[288, 128], [1, 8], [0, 8]])
            v.wait("sv")
            for tc in range(TC):
                v.reduce_max(rwk[:, 256 + tc:257 + tc], rwk[:, tc * 8:(tc + 1) * 8],
                             axis=AX, negate=True).inc("sv", 1)
            v.wait("sv")
            v.tensor_tensor(rwk[:, 64:128], rwk[:, 0:64], rep8(256), ALU.add).inc("sv", 1)
            v.wait("sv")
            v.tensor_scalar(rwk[:, 128:192], rwk[:, 64:128], 0.0, None, ALU.is_ge).inc("sv", 1)
            s.wait("sv")
            s.activation(rwk[:, 192:256], rwk[:, 64:128], AF.Exp).inc("sc", 1)
            v.wait("sv")
            v.scalar_tensor_tensor(rwk[:, 0:64], rwk[:, 128:192], -1e30, rwk[:, 64:128],
                                   ALU.mult, ALU.add).inc("sv", 1)
            v.wait("sv")
            for tc in range(TC):
                v.reduce_max(rwk[:, 264 + tc:265 + tc], rwk[:, tc * 8:(tc + 1) * 8],
                             axis=AX, negate=True).inc("sv", 1)
            v.wait("sv")
            v.tensor_tensor(rwk[:, 0:64], rwk[:, 64:128], rep8(264), ALU.add).inc("sv", 1)
            v.wait("sv")
            v.tensor_scalar(rwk[:, 128:192], rwk[:, 0:64], 0.0, None, ALU.is_ge).inc("sv", 1)
            v.wait("sv")
            v.wait("sc")
            v.tensor_mul(rwk[:, 64:128], rwk[:, 128:192], rwk[:, 192:256]).inc("sv", 1)
            v.wait("sv")
            for tc in range(TC):
                v.reduce_sum(rwk[:, 272 + tc:273 + tc], rwk[:, 64 + tc * 8:64 + (tc + 1) * 8],
                             axis=AX).inc("sv", 1)
            v.wait("sv")
            v.reciprocal(rwk[:, 280:288], rwk[:, 272:280]).inc("sv", 1)
            v.wait("sv")
            v.scalar_tensor_tensor(cmbt[:, 0:64], rwk[:, 64:128], 2.0, rep8(280),
                                   ALU.mult, ALU.mult).inc("sv", 1)
            cmbdone = C["sv"]
            t.wait("sw", wa_cnt)
            t.wait("sv", lgcopy)
            hdone = [0] * TC
            for tc in range(TC):
                for kc in range(KC):
                    h_ = t.matmul(pf[tc][:, 0:128], R1t(kc, tc),
                                  wst[:, kc * 128:(kc + 1) * 128],
                                  start=(kc == 0), stop=(kc == 31))
                    if kc == 31:
                        h_.inc("sm", 1)
                hdone[tc] = C["sm"]
            hw_cnt = [0] * TC
            tr_cnt = [0] * TC
            for tc in range(TC):
                v.wait("sm", hdone[tc])
                v.wait("sv", cmbdone)
                if tc >= 2:
                    v.wait("sm", tr_cnt[tc - 2])
                for e in range(8):
                    v.tensor_scalar(hwf[:, (tc % 2) * 128 + e * 16:
                                        (tc % 2) * 128 + (e + 1) * 16],
                                    pf[tc][:, e * 16:(e + 1) * 16],
                                    cmbt[:, tc * 8 + e: tc * 8 + e + 1], None,
                                    ALU.mult).inc("sv", 1)
                hw_cnt[tc] = C["sv"]
                t.wait("sv", hw_cnt[tc])
                t.transpose(pf[tc][:, 128:256], hwf[:, (tc % 2) * 128:(tc % 2 + 1) * 128],
                            idb[:]).inc("sm", 1)
                tr_cnt[tc] = C["sm"]
                v.wait("sm", tr_cnt[tc])
                v.tensor_copy(hwT[:, tc * 128:(tc + 1) * 128],
                              pf[tc][:, 128:256]).inc("sv", 1)
            barrier(gx=[])

            p5_sm0 = C.get("sm", 0)
            sqcnt = [0] * TC
            gldc = [0] * TC
            for half in range(2):
                s.activation(xbuf[:], R2[:, half * 2048:half * 2048 + 2048],
                             AF.Square, accum_out=rv1[:, half:half + 1]).inc("sc", 1)
            sqcnt[0] = C["sc"]
            y.dma_start(wst[:, 0:4096], gscr.ap()[0:128, :]).inc("sx", 16)
            gldc[0] = C["sx"]
            for tc in range(TC):
                v.wait("sc", sqcnt[tc])
                v.tensor_tensor(stt_[:, 8:9], rv1[:, 2 * (tc % 4):2 * (tc % 4) + 1],
                                rv1[:, 2 * (tc % 4) + 1:2 * (tc % 4) + 2], ALU.add).inc("sv", 1)
                v.wait("sv")
                v.tensor_scalar(stt_[:, 8:9], stt_[:, 8:9], 1.0 / DM, 1e-5,
                                ALU.mult, ALU.add).inc("sv", 1)
                v.wait("sv")
                v.reciprocal(stt_[:, 9:10], stt_[:, 8:9]).inc("sv", 1)
                s.wait("sv")
                s.activation(stt_[:, 10:11], stt_[:, 9:10], AF.Sqrt).inc("sc", 1)
                v.wait("sc")
                ghd = [0, 0]
                for half in range(2):
                    if tc > 0:
                        v.wait("sm", p5_sm0 + (tc - 1) * 32 + (half + 1) * 16)
                    v.wait("sv")
                    v.scalar_tensor_tensor(stage[:, half * 2048:(half + 1) * 2048],
                                           R2[:, tc * 4096 + half * 2048:
                                              tc * 4096 + (half + 1) * 2048],
                                           stt_[:, 10:11],
                                           nwb[:, half * 2048:(half + 1) * 2048],
                                           ALU.mult, ALU.mult).inc("sv", 1)
                    if half == 0:
                        v.wait("sx", gldc[tc])
                    v.wait("sv")
                    v.tensor_mul(stage[:, half * 2048:(half + 1) * 2048],
                                 stage[:, half * 2048:(half + 1) * 2048],
                                 wst[:, half * 2048:(half + 1) * 2048]).inc("sv", 1)
                    ghd[half] = C["sv"]
                if tc + 1 < TC:
                    y.wait("sv", ghd[1])
                    y.dma_start(wst[:, 0:4096],
                                gscr.ap()[(tc + 1) * 128:(tc + 2) * 128, :]).inc("sx", 16)
                    gldc[tc + 1] = C["sx"]
                if tc + 1 < TC:
                    for half in range(2):
                        s.wait("sv", ghd[0])
                        s.activation(xbuf[:],
                                     R2[:, (tc + 1) * 4096 + half * 2048:
                                        (tc + 1) * 4096 + (half + 1) * 2048],
                                     AF.Square,
                                     accum_out=rv1[:, 2 * ((tc + 1) % 4) + half:
                                                   2 * ((tc + 1) % 4) + half + 1]).inc("sc", 1)
                    sqcnt[tc + 1] = C["sc"]
                scb = C.get("sc", 0)
                for i in range(KC):
                    if i >= 8:
                        t.wait("sc", scb + i - 7)
                        t.wait("sv", ghd[1] if i >= 16 else ghd[0])
                    else:
                        t.wait("sc", scb)
                        t.wait("sv", ghd[0])
                    t.transpose(pf[i % 8][:, 0:128],
                                stage[:, i * 128:(i + 1) * 128],
                                idb[:]).inc("sm", 1)
                    s.wait("sm")
                    s.activation(R2[:, tc * 4096 + i * 128: tc * 4096 + (i + 1) * 128],
                                 pf[i % 8][:, 0:128], AF.Copy).inc("sc", 1)
            barrier()

            p6_smh0 = C.get("smh", 0)
            p6_smb0 = C.get("smb_", 0)
            ev0 = C.get("sv", 0)
            so0 = C.get("so", 0)
            gi = 0
            bbld = [0] * 8
            pend = []
            def flush_out():
                for (fo_, tc_, evcnt) in pend:
                    g.wait("sv", evcnt)
                    if fo_ == 0 and tc_ == 0:
                        g.wait("ss", xcopy)
                    g.dma_start(out_h.ap()[tc_ * 128:(tc_ + 1) * 128,
                                           fo_ * 512:(fo_ + 1) * 512],
                                stage[:, tc_ * 512:(tc_ + 1) * 512],
                                accum_op=ALU.add).inc("so", 16)
                pend.clear()
            for fo in range(8):
                if fo >= 2:
                    g.wait("smb_", p6_smb0 + (fo - 1) * 8)
                g.dma_start(bst[:, (fo % 2) * 512:(fo % 2) * 512 + 512],
                            wbf_h.ap()[:, fo * 512:(fo + 1) * 512]).inc("sw", 16)
                bcnt = C["sw"]
                if fo >= 4:
                    g.wait("sv", ev0 + (fo - 4) * 8 + 8)
                g.dma_start(xbuf[:, (fo % 4) * 512:(fo % 4) * 512 + 512],
                            bass.AP(bb_h, fo * 512, [[0, 128], [1, 512]])).inc("sw", 16)
                bbld[fo] = C["sw"]
                for w_i, w_hh in enumerate((wo_h, wb_h)):
                    for kg in range(16):
                        slot = gi % 4
                        if gi >= 4:
                            g.wait("smh", p6_smh0 + gi - 3)
                        g.dma_start(
                            wst[:, slot * 1024:(slot + 1) * 1024],
                            bass.AP(w_hh, (2 * kg * 128) * DM + fo * 512,
                                    [[DM, 128], [128 * DM, 2], [1, 512]])).inc("sw", 16)
                        gcnt = C["sw"]
                        if w_i == 0 and kg == 4:
                            flush_out()
                        t.wait("sw", gcnt)
                        for j in range(2):
                            kc = 2 * kg + j
                            for tc in range(TC):
                                if w_i == 0 and kc == 0 and fo > 0:
                                    t.wait("sv", ev0 + (fo - 1) * 8 + tc + 1)
                                lhsT = (R2[:, tc * 4096 + kc * 128: tc * 4096 + (kc + 1) * 128]
                                        if w_i == 0 else R1t(kc, tc))
                                h_ = t.matmul(pf[tc][:], lhsT,
                                              wst[:, slot * 1024 + j * 512:
                                                  slot * 1024 + (j + 1) * 512],
                                              start=(w_i == 0 and kc == 0), stop=False)
                                if tc == TC - 1 and j == 1:
                                    h_.inc("smh", 1)
                        gi += 1
                t.wait("sw", bcnt)
                for tc in range(TC):
                    t.matmul(pf[tc][:], hwT[:, tc * 128:(tc + 1) * 128],
                             bst[:, (fo % 2) * 512:(fo % 2) * 512 + 512],
                             start=False, stop=True).inc("smb_", 1)
                for tc in range(TC):
                    v.wait("smb_")
                    v.wait("sw", bbld[fo])
                    if fo > 0:
                        v.wait("so", so0 + 16 * ((fo - 1) * 8 + tc + 1))
                    v.tensor_tensor(stage[:, tc * 512:(tc + 1) * 512], pf[tc][:],
                                    xbuf[:, (fo % 4) * 512:(fo % 4) * 512 + 512],
                                    ALU.add).inc("sv", 1)
                    pend.append((fo, tc, C["sv"]))
            flush_out()
            g.wait("so")
            y.wait("so")

        def run_engine(name):
            def fn(engine):
                env = _Env()
                env.cnt = {}
                env.sems = sems
                for nm in ("t", "v", "s", "g", "y"):
                    setattr(env, nm, _P(env, engine if nm == name else None))
                emit_all(env)
            return fn

        blk.tensor(run_engine("t"))
        blk.vector(run_engine("v"))
        blk.scalar(run_engine("s"))
        blk.gpsimd(run_engine("g"))
        blk.sync(run_engine("y"))

    return nc


def kernel(L0_attn_hidden_states, L0_attn_qkv_w, L0_attn_out_w, L0_attn_gate_w,
           L0_attn_norm_w, L0_ffn_W_base, L0_ffn_b_base, L0_ffn_W_router,
           L0_ffn_A, L0_ffn_B):
    x = np.asarray(L0_attn_hidden_states, dtype=np.float32)
    qkv_w = np.asarray(L0_attn_qkv_w, dtype=np.float32)
    out_w = np.asarray(L0_attn_out_w, dtype=np.float32)
    gate_w = np.asarray(L0_attn_gate_w, dtype=np.float32)
    norm_w = np.asarray(L0_attn_norm_w, dtype=np.float32)
    W_base = np.asarray(L0_ffn_W_base, dtype=np.float32)
    b_base = np.asarray(L0_ffn_b_base, dtype=np.float32)
    W_router = np.asarray(L0_ffn_W_router, dtype=np.float32)
    A = np.asarray(L0_ffn_A, dtype=np.float32)
    B = np.asarray(L0_ffn_B, dtype=np.float32)

    slope, qd64, kd64, dd64, bd64 = _decays()

    qk_w = np.empty((8192, DM), np.float32)
    v_w = np.empty((DM, DM), np.float32)
    for h in range(NH):
        qk_w[256 * h:256 * h + 128] = qkv_w[384 * h:384 * h + 128]
        qk_w[256 * h + 128:256 * h + 256] = qkv_w[384 * h + 128:384 * h + 256]
        v_w[128 * h:128 * (h + 1)] = qkv_w[384 * h + 256:384 * h + 384]
    wqk = np.ascontiguousarray(qk_w.reshape(64, 128, KC, 128).transpose(0, 3, 2, 1))
    wv = np.ascontiguousarray(v_w.T)
    wg = np.ascontiguousarray(gate_w.T)
    wo = np.ascontiguousarray(out_w.T)
    wb = np.ascontiguousarray(W_base.T)
    wr = np.ascontiguousarray(W_router.T.reshape(KC, 128, 8).transpose(1, 0, 2).reshape(128, 256))
    wa = np.ascontiguousarray(A.transpose(2, 0, 1).reshape(DM, 128))
    wbf = np.ascontiguousarray(B.transpose(0, 2, 1).reshape(128, DM))

    ddt = np.empty((NH, 2, 128, 256), np.float32)
    for h in range(NH):
        for mc in range(2):
            ddt[h, mc] = dd64[h][:, mc * 128:(mc + 1) * 128].T.astype(np.float32)
    kdt = np.empty((128, 64), np.float32)
    qdt = np.empty((128, 64), np.float32)
    for h in range(NH):
        for mc in range(2):
            kdt[:, h * 2 + mc] = kd64[h, mc * 128:(mc + 1) * 128]
            qdt[:, h * 2 + mc] = qd64[h, mc * 128:(mc + 1) * 128]
    idm = np.eye(128, dtype=np.float32)

    shared = dict(wqk=wqk, wv=wv, wg=wg, wo=wo, wb=wb, wr=wr, wa=wa, wbf=wbf,
                  nw=norm_w, bbv=b_base, ddt=ddt, kdt=kdt, qdt=qdt, idm=idm)
    in_maps = []
    for c in range(8):
        b, q = c // 4, c % 4
        cwm = np.zeros((NH, 4), np.float32)
        for r in range(q):
            cwm[:, r] = np.exp(-256.0 * slope * 4 * (q - 1 - r)).astype(np.float32)
        m = dict(shared)
        m["x"] = np.ascontiguousarray(x[b, q * TOK:(q + 1) * TOK, :])
        m["cw"] = cwm
        in_maps.append(m)

    nc = build_nc()
    res = run_bass_kernel_spmd(nc, in_maps, core_ids=list(range(8)))
    out = np.empty((2, 4096, DM), np.float32)
    for c in range(8):
        b, q = c // 4, c % 4
        out[b, q * TOK:(q + 1) * TOK, :] = res.results[c]["out"]
    return out


if __name__ == "__main__":
    import reference as _r
    inp = _r.setup_inputs()
    o = kernel(**{k: np.asarray(v) for k, v in inp.items()})
    ref = np.asarray(_r.reference(**inp))
    err = np.mean((o - ref) ** 2) / np.mean(ref ** 2)
    print("resid var ratio", err)


# version 13
# speedup vs baseline: 1.0059x; 1.0059x over previous
import numpy as np
import ml_dtypes
from contextlib import ExitStack
import concourse.bass as bass
import concourse.mybir as mybir
from concourse.bass_utils import run_bass_kernel_spmd

BF = ml_dtypes.bfloat16
F32, BF16 = mybir.dt.float32, mybir.dt.bfloat16
AF = mybir.ActivationFunctionType
ALU = mybir.AluOpType
AX = mybir.AxisListType.X

NH, HD, BLKSZ = 32, 128, 256
TOK = 1024
TC = 8
KC = 32
NJ = 4
DM = 4096


def _decays():
    base = np.float64(2.0) ** np.float64(-0.25)
    exponent = np.arange(NH, dtype=np.float64) + 1.0
    factor = np.float64(1.0) - np.float64(1.0) / (31 + 1e-5) + np.float64(1e-5)
    slope = base ** exponent * factor
    r = np.arange(BLKSZ, dtype=np.float64) + 1.0
    qd = np.exp(-slope[:, None] * r[None, :])
    kd = np.exp(-slope[:, None] * (BLKSZ - r)[None, :])
    diff = r[:, None] - r[None, :]
    dd = np.where(diff >= 0, np.exp(-slope[:, None, None] * diff[None]), 0.0)
    bd = np.exp(-slope * BLKSZ)
    return slope, qd, kd, dd, bd


class _H:
    __slots__ = ("env", "ins")
    def __init__(self, env, ins):
        self.env, self.ins = env, ins
    def inc(self, semname, k):
        self.env.cnt[semname] = self.env.cnt.get(semname, 0) + k
        if self.ins is not None:
            self.ins.then_inc(self.env.sems[semname], k)
        return self


class _P:
    def __init__(self, env, real):
        self._env, self._real = env, real
    def __getattr__(self, name):
        env, real = self._env, self._real
        def call(*a, **k):
            ins = getattr(real, name)(*a, **k) if real is not None else None
            return _H(env, ins)
        return call
    def wait(self, semname, val=None):
        v = self._env.cnt.get(semname, 0) if val is None else val
        if v <= 0:
            return
        if self._real is not None:
            self._real.wait_ge(self._env.sems[semname], v)


class _Env:
    pass


def build_nc():
    nc = bass.Bass("TRN2", target_bir_lowering=False)

    x_h = nc.dram_tensor("x", [TOK, DM], F32, kind="ExternalInput")
    wqk_h = nc.dram_tensor("wqk", [64, 128, KC, 128], F32, kind="ExternalInput")
    wv_h = nc.dram_tensor("wv", [DM, DM], F32, kind="ExternalInput")
    wg_h = nc.dram_tensor("wg", [DM, DM], F32, kind="ExternalInput")
    wo_h = nc.dram_tensor("wo", [DM, DM], F32, kind="ExternalInput")
    wb_h = nc.dram_tensor("wb", [DM, DM], F32, kind="ExternalInput")
    wr_h = nc.dram_tensor("wr", [128, 256], F32, kind="ExternalInput")
    wa_h = nc.dram_tensor("wa", [DM, 128], F32, kind="ExternalInput")
    wbf_h = nc.dram_tensor("wbf", [128, DM], F32, kind="ExternalInput")
    nw_h = nc.dram_tensor("nw", [DM], F32, kind="ExternalInput")
    bb_h = nc.dram_tensor("bbv", [DM], F32, kind="ExternalInput")
    ddt_h = nc.dram_tensor("ddt", [NH, 2, 128, 256], F32, kind="ExternalInput")
    kdt_h = nc.dram_tensor("kdt", [128, 64], F32, kind="ExternalInput")
    qdt_h = nc.dram_tensor("qdt", [128, 64], F32, kind="ExternalInput")
    cw_h = nc.dram_tensor("cw", [NH, 4], F32, kind="ExternalInput")
    idm_h = nc.dram_tensor("idm", [128, 128], F32, kind="ExternalInput")
    out_h = nc.dram_tensor("out", [TOK, DM], F32, kind="ExternalOutput")
    qkscr = nc.dram_tensor("qkscr", [64, 128, 1024], BF16)
    gscr = nc.dram_tensor("gscr", [TOK, DM], BF16)
    cl_h = [nc.dram_tensor(f"cl{h}", [128, 128], F32) for h in range(NH)]
    ca_h = [nc.dram_tensor(f"ca{h}", [4, 128, 128], F32) for h in range(NH)]

    _, _, _, _, bd64 = _decays()
    bdf = [float(np.float32(v)) for v in bd64]

    with ExitStack() as ctx:
        sb = lambda n, s, d: ctx.enter_context(nc.sbuf_tensor(n, s, d))
        psf = lambda n: ctx.enter_context(nc.psum_tensor(n, [128, 512], F32))
        R1 = sb("R1", [128, 32768], BF16)
        R2 = sb("R2", [128, 32768], BF16)
        xbuf = sb("xbuf", [128, 2048], F32)
        stage = sb("stage", [128, 4096], F32)
        wst = sb("wst", [128, 4096], BF16)
        evq = sb("evq", [128, 2048], BF16)
        nwb = sb("nwb", [128, 4096], BF16)
        qh2 = sb("qh2", [128, 2048], BF16)
        ddb = sb("ddb", [128, 512], F32)
        smb = sb("smb", [128, 2048], BF16)
        ckt = sb("ckt", [128, 1024], BF16)
        cvp = sb("cvp", [128, 1024], BF16)
        cbuf = sb("cbuf", [128, 1024], BF16)
        sfall = sb("sfall", [128, 512], F32)
        sfa = sb("sfa", [128, 128], F32)
        sba = sb("sba", [128, 1024], BF16)
        lca = sb("lca", [128, 128], F32)
        lcb = sb("lcb", [128, 128], F32)
        kdc = sb("kdc", [128, 64], F32)
        qdc = sb("qdc", [128, 64], F32)
        cwb = sb("cwb", [128, 128], F32)
        crd = sb("crd", [128, 1024], F32)
        cmbt = sb("cmbt", [128, 64], F32)
        hwf = sb("hwf", [128, 256], F32)
        hwT = sb("hwT", [128, 1024], BF16)
        idb = sb("idb", [128, 128], F32)
        stt_ = sb("stats", [128, 16], F32)
        rwk = sb("rwk", [128, 288], F32)
        rv1 = sb("rv1", [128, 8], F32)
        bst = sb("bst", [128, 1024], BF16)
        wrb = sb("wrb", [128, 256], BF16)
        pf = [psf(f"pf{i}") for i in range(8)]

        semnames = ["sx", "sw", "sp", "sd", "ss", "sm", "smh", "smb_", "sv",
                    "sc", "scc", "so", "st_"]
        sems = {n: ctx.enter_context(nc.semaphore(n)) for n in semnames}
        blk = ctx.enter_context(nc.Block())

        def R1xn(kc, half):
            return R1[:, kc * 1024 + half * 512: kc * 1024 + (half + 1) * 512]

        def R1t(kc, tc):
            return R1[:, kc * 1024 + tc * 128: kc * 1024 + (tc + 1) * 128]

        def emit_all(env):
            t, v, s, g, y = env.t, env.v, env.s, env.g, env.y
            C = env.cnt

            def barrier(gx=None):
                for e in (t, v, s, g, y):
                    for sn in semnames:
                        if sn == "scc" or C.get(sn, 0) <= 0:
                            continue
                        if e is g and gx is not None and sn not in gx:
                            continue
                        e.wait(sn)

            g.dma_start(idb[:], idm_h.ap()).inc("sp", 16)
            g.dma_start(nwb[:], bass.AP(nw_h, 0, [[0, 128], [1, 4096]])).inc("sp", 16)
            g.dma_start(kdc[:], kdt_h.ap()).inc("sp", 16)
            g.dma_start(qdc[:], qdt_h.ap()).inc("sp", 16)
            g.dma_start(cwb[:], bass.AP(cw_h, 0, [[0, 128], [1, 128]])).inc("sp", 16)
            g.dma_start(wrb[:], wr_h.ap()).inc("sp", 16)
            for e in (t, v, s):
                e.wait("sp")

            stage_free = 0
            for tc in range(TC):
                half_sx = [0, 0]
                for half in range(2):
                    if half == 0:
                        y.wait("sv", stage_free)
                    y.dma_start(stage[:, half * 2048:(half + 1) * 2048],
                                x_h.ap()[tc * 128:(tc + 1) * 128,
                                         half * 2048:(half + 1) * 2048]).inc("sx", 16)
                    half_sx[half] = C["sx"]
                    v.wait("sx", half_sx[half])
                    v.reduce_sum(stt_[:, 12 + half:13 + half],
                                 stage[:, half * 2048:(half + 1) * 2048], axis=AX).inc("sv", 1)
                    s.wait("sx", half_sx[half])
                    s.activation(R2[:, 0:2048], stage[:, half * 2048:(half + 1) * 2048],
                                 AF.Square, accum_out=stt_[:, 14 + half:15 + half]).inc("sc", 1)
                v.wait("sv")
                v.tensor_tensor(stt_[:, 0:1], stt_[:, 12:13], stt_[:, 13:14], ALU.add).inc("sv", 1)
                v.wait("sv")
                v.tensor_scalar(stt_[:, 1:2], stt_[:, 0:1], -1.0 / DM, None, ALU.mult).inc("sv", 1)
                v.wait("sv")
                v.tensor_mul(stt_[:, 2:3], stt_[:, 1:2], stt_[:, 1:2]).inc("sv", 1)
                v.wait("sc")
                v.wait("sv")
                v.tensor_tensor(stt_[:, 3:4], stt_[:, 14:15], stt_[:, 15:16], ALU.add).inc("sv", 1)
                v.wait("sv")
                v.scalar_tensor_tensor(stt_[:, 4:5], stt_[:, 3:4], 1.0 / DM, stt_[:, 2:3],
                                       ALU.mult, ALU.subtract).inc("sv", 1)
                v.wait("sv")
                v.tensor_scalar(stt_[:, 4:5], stt_[:, 4:5], 1e-5, None, ALU.add).inc("sv", 1)
                v.wait("sv")
                v.reciprocal(stt_[:, 5:6], stt_[:, 4:5]).inc("sv", 1)
                s.wait("sv")
                s.activation(stt_[:, 6:7], stt_[:, 5:6], AF.Sqrt).inc("sc", 1)
                for half in range(2):
                    v.wait("sc")
                    if tc > 0 or half > 0:
                        v.wait("sm")
                    v.tensor_scalar(xbuf[:], stage[:, half * 2048:(half + 1) * 2048],
                                    stt_[:, 1:2], stt_[:, 6:7],
                                    ALU.add, ALU.mult).inc("sv", 1)
                    xnh = C["sv"]
                    if half == 1:
                        stage_free = xnh
                    for kcl in range(16):
                        kc = half * 16 + kcl
                        if kcl >= 8:
                            t.wait("sv", xnh + kcl - 7)
                        else:
                            t.wait("sv", xnh)
                        t.transpose(pf[kc % 8][:, 0:128], xbuf[:, kcl * 128:(kcl + 1) * 128],
                                    idb[:]).inc("sm", 1)
                        v.wait("sm")
                        v.tensor_copy(R1[:, kc * 1024 + tc * 128: kc * 1024 + (tc + 1) * 128],
                                      pf[kc % 8][:, 0:128]).inc("sv", 1)
            barrier(gx=[])

            def aform_real(w_hh, evac_cb):
                base_smh = C.get("smh", 0)
                base_sw = C.get("sw", 0)
                base_sc = C.get("sc", 0)
                gi = 0
                for fo in range(8):
                    for kg in range(16):
                        if gi >= 4:
                            g.wait("smh", base_smh + gi - 3)
                        g.dma_start(
                            wst[:, (gi % 4) * 1024:(gi % 4) * 1024 + 1024],
                            bass.AP(w_hh, (2 * kg * 128) * DM + fo * 512,
                                    [[DM, 128], [128 * DM, 2], [1, 512]])).inc("sw", 16)
                        t.wait("sw", base_sw + 16 * (gi + 1))
                        for j in range(2):
                            kc = 2 * kg + j
                            for tc in range(TC):
                                if kc == 0 and fo > 0:
                                    t.wait("sc", base_sc + (fo - 1) * 8 + tc + 1)
                                h_ = t.matmul(pf[tc][:], R1t(kc, tc),
                                              wst[:, (gi % 4) * 1024 + j * 512:
                                                  (gi % 4) * 1024 + (j + 1) * 512],
                                              start=(kc == 0), stop=(kc == 31))
                                if tc == TC - 1 and j == 1:
                                    h_.inc("smh", 1)
                        gi += 1
                    for tc in range(TC):
                        evac_cb(fo, tc)

            p1_smh = C.get("smh", 0)
            def v_evac(fo, tc):
                s.wait("smh", p1_smh + (fo + 1) * 16)
                s.activation(R2[:, tc * 4096 + fo * 512: tc * 4096 + (fo + 1) * 512],
                             pf[tc][:], AF.Silu).inc("sc", 1)
            aform_real(wv_h, v_evac)
            barrier(gx=["smh"])

            p2_smh0 = C.get("smh", 0)
            p2_sw0 = C.get("sw", 0)
            ST = {"hi": 0}
            Z = -1
            dd_cnt = [0] * NH; kspill = [0] * NH; scores_cnt = [0] * NH
            mask_cnt = [0] * NH; cmm = [0] * NH; intra_ev = [0] * NH
            horner = [0] * NH; carry_ss = [0] * NH; crd_cnt = [0] * NH
            qld = [0] * NH; cast_cnt = [0] * NH; inter_mm = [0] * NH
            schain = [0] * NH; qevac = [0] * NH; kevac = [0] * NH
            ck_cnt = [0] * NH; cvp_cnt = [0] * NH

            def qk_chunk(h, piece):
                c = 2 * h + piece
                pi = 2 * piece
                for half in range(2):
                    hi = ST["hi"]
                    if hi >= 2:
                        g.wait("smh", p2_smh0 + hi - 1)
                    g.dma_start(
                        wst[:, (hi % 2) * 2048:(hi % 2) * 2048 + 2048],
                        bass.AP(wqk_h, c * 128 * DM + half * 2048,
                                [[DM, 128], [1, 2048]])).inc("sw", 16)
                    t.wait("sw", p2_sw0 + 16 * (hi + 1))
                    if half == 0 and h > 0:
                        t.wait("sc", (qevac if piece == 0 else kevac)[h - 1])
                    for kcl in range(16):
                        kc = half * 16 + kcl
                        lhsT = wst[:, (hi % 2) * 2048 + kcl * 128:
                                   (hi % 2) * 2048 + (kcl + 1) * 128]
                        t.matmul(pf[pi][:], lhsT, R1xn(kc, 0),
                                 start=(kc == 0), stop=(kc == 31))
                        h1 = t.matmul(pf[pi + 1][:], lhsT, R1xn(kc, 1),
                                      start=(kc == 0), stop=(kc == 31))
                        if kcl == 15:
                            h1.inc("smh", 1)
                    ST["hi"] += 1
                s.wait("smh")
                for bhalf in range(2):
                    s.activation(evq[:, piece * 1024 + bhalf * 512:
                                     piece * 1024 + (bhalf + 1) * 512],
                                 pf[pi + bhalf][:], AF.Silu).inc("sc", 1)
                if piece == 0:
                    qevac[h] = C["sc"]
                else:
                    kevac[h] = C["sc"]
                y.wait("sc")
                y.dma_start(qkscr.ap()[c, :, :],
                            evq[:, piece * 1024:(piece + 1) * 1024]).inc("ss", 16)
                if piece == 1:
                    kspill[h] = C["ss"]

            for hh in range(NH + 3):
                h1_ = hh - 1
                h2_ = hh - 2
                h3_ = hh - 3
                if 0 <= h1_ < NH:
                    v.wait("sm", scores_cnt[h1_])
                    v.wait("sd", dd_cnt[h1_])
                    for j in range(NJ):
                        for mc in range(2):
                            v.tensor_mul(smb[:, j * 512 + mc * 256: j * 512 + (mc + 1) * 256],
                                         pf[4 + j][:, mc * 256:(mc + 1) * 256],
                                         ddb[:, mc * 256:(mc + 1) * 256]).inc("sv", 1)
                    mask_cnt[h1_] = C["sv"]
                if hh < NH:
                    if h1_ >= 0:
                        y.wait("sv", mask_cnt[h1_])
                    y.dma_start(ddb[:], bass.AP(ddt_h, hh * 65536,
                                                [[256, 128], [32768, 2], [1, 256]])).inc("sd", 16)
                    dd_cnt[hh] = C["sd"]
                if 0 <= h2_ < NH:
                    g.wait("ss", carry_ss[h2_])
                    g.collective_compute("AllGather", ALU.bypass,
                                         replica_groups=[[0, 1, 2, 3], [4, 5, 6, 7]],
                                         ins=[cl_h[h2_].ap()],
                                         outs=[ca_h[h2_].ap()]).inc("scc", 1)
                    if h2_ >= 2:
                        y.wait("sm", inter_mm[h2_ - 2])
                    y.dma_start(qh2[:, (h2_ % 2) * 1024:(h2_ % 2 + 1) * 1024],
                                qkscr.ap()[2 * h2_, :, :]).inc("sx", 16)
                    qld[h2_] = C["sx"]
                if 0 <= h3_ < NH:
                    s.wait("scc", h3_ + 1)
                    if h3_ >= 2:
                        s.wait("sv", schain[h3_ - 2])
                    s.dma_start(crd[:, (h3_ % 2) * 512:(h3_ % 2) * 512 + 512],
                                bass.AP(ca_h[h3_], 0,
                                        [[128, 128], [16384, 4], [1, 128]])).inc("st_", 16)
                    crd_cnt[h3_] = C["st_"]
                if 0 <= h1_ < NH:
                    s.wait("ss", kspill[h1_])
                    if h1_ >= 1:
                        s.wait("sm", cmm[h1_ - 1])
                    for j in range(NJ):
                        for mc in range(2):
                            s.dma_start_transpose(
                                ckt[:, j * 256 + mc * 128: j * 256 + (mc + 1) * 128],
                                qkscr.ap()[2 * h1_ + 1, :,
                                           j * 256 + mc * 128: j * 256 + (mc + 1) * 128]
                            ).inc("st_", 16)
                    ck_cnt[h1_] = C["st_"]
                if 0 <= h3_ < NH:
                    p3_ = h3_ % 2
                    v.wait("st_", crd_cnt[h3_])
                    v.tensor_scalar(sfa[:], crd[:, p3_ * 512: p3_ * 512 + 128],
                                    cwb[:, h3_ * 4: h3_ * 4 + 1], None, ALU.mult).inc("sv", 1)
                    v.wait("sv")
                    v.scalar_tensor_tensor(sfall[:, 0:128],
                                           crd[:, p3_ * 512 + 128: p3_ * 512 + 256],
                                           cwb[:, h3_ * 4 + 1: h3_ * 4 + 2], sfa[:],
                                           ALU.mult, ALU.add).inc("sv", 1)
                    v.wait("sv")
                    v.scalar_tensor_tensor(sfa[:],
                                           crd[:, p3_ * 512 + 256: p3_ * 512 + 384],
                                           cwb[:, h3_ * 4 + 2: h3_ * 4 + 3],
                                           sfall[:, 0:128], ALU.mult, ALU.add).inc("sv", 1)
                    v.wait("sv")
                    v.scalar_tensor_tensor(sfall[:, 0:128],
                                           crd[:, p3_ * 512 + 384: p3_ * 512 + 512],
                                           cwb[:, h3_ * 4 + 3: h3_ * 4 + 4], sfa[:],
                                           ALU.mult, ALU.add).inc("sv", 1)
                    for j in range(1, NJ):
                        v.wait("sv")
                        v.scalar_tensor_tensor(sfall[:, j * 128:(j + 1) * 128],
                                               sfall[:, (j - 1) * 128: j * 128], bdf[h3_],
                                               cbuf[:, p3_ * 512 + (j - 1) * 128:
                                                    p3_ * 512 + j * 128],
                                               ALU.mult, ALU.add).inc("sv", 1)
                    schain[h3_] = C["sv"]
                    s.wait("sv", schain[h3_])
                    if h3_ >= 2:
                        s.wait("sm", inter_mm[h3_ - 2])
                    s.activation(sba[:, p3_ * 512:(p3_ + 1) * 512], sfall[:],
                                 AF.Copy).inc("sc", 1)
                    cast_cnt[h3_] = C["sc"]
                if hh < NH:
                    qk_chunk(hh, 0)
                if 0 <= h1_ < NH:
                    t.wait("sv", mask_cnt[h1_])
                    t.wait("st_", ck_cnt[h1_])
                    for j in range(NJ):
                        bank = pf[4 + j]
                        for nc_ in range(2):
                            for mc in range(2):
                                t.matmul(bank[:, nc_ * 128:(nc_ + 1) * 128],
                                         smb[:, j * 512 + mc * 256 + nc_ * 128:
                                             j * 512 + mc * 256 + (nc_ + 1) * 128],
                                         R2[:, (2 * j + mc) * 4096 + h1_ * 128:
                                            (2 * j + mc) * 4096 + (h1_ + 1) * 128],
                                         start=(mc == 0), stop=(mc == 1))
                        for mc in range(2):
                            h_ = t.matmul(bank[:, 256:384],
                                          ckt[:, j * 256 + mc * 128: j * 256 + (mc + 1) * 128],
                                          cvp[:, j * 256 + mc * 128: j * 256 + (mc + 1) * 128],
                                          start=(mc == 0), stop=(mc == 1))
                        h_.inc("sm", 1)
                    cmm[h1_] = C["sm"]
                    p1_ = h1_ % 2
                    for j in range(NJ):
                        bank = pf[4 + j]
                        v.wait("sm", cmm[h1_] - (NJ - 1 - j))
                        for nc_ in range(2):
                            v.tensor_copy(R2[:, (2 * j + nc_) * 4096 + h1_ * 128:
                                             (2 * j + nc_) * 4096 + (h1_ + 1) * 128],
                                          bank[:, nc_ * 128:(nc_ + 1) * 128]).inc("sv", 1)
                        v.tensor_copy(cbuf[:, p1_ * 512 + j * 128: p1_ * 512 + (j + 1) * 128],
                                      bank[:, 256:384]).inc("sv", 1)
                        if j == 0:
                            v.tensor_copy(lca[:], bank[:, 256:384]).inc("sv", 1)
                        else:
                            lsrc, ldst = (lca, lcb) if j % 2 == 1 else (lcb, lca)
                            if j == 1 and h1_ >= 1:
                                v.wait("ss", carry_ss[h1_ - 1])
                            v.wait("sv")
                            v.scalar_tensor_tensor(ldst[:], lsrc[:], bdf[h1_],
                                                   bank[:, 256:384],
                                                   ALU.mult, ALU.add).inc("sv", 1)
                    intra_ev[h1_] = C["sv"]
                    horner[h1_] = C["sv"]
                    y.wait("sv", horner[h1_])
                    y.dma_start(cl_h[h1_].ap(), lcb[:]).inc("ss", 16)
                    carry_ss[h1_] = C["ss"]
                if 0 <= h3_ < NH:
                    p3_ = h3_ % 2
                    t.wait("sc", cast_cnt[h3_])
                    t.wait("sx", qld[h3_])
                    if 0 <= h1_ < NH:
                        t.wait("sv", intra_ev[h1_])
                    for j in range(NJ):
                        bank = pf[4 + j]
                        for nc_ in range(2):
                            h_ = t.matmul(bank[:, nc_ * 128:(nc_ + 1) * 128],
                                          qh2[:, p3_ * 1024 + j * 256 + nc_ * 128:
                                              p3_ * 1024 + j * 256 + (nc_ + 1) * 128],
                                          sba[:, p3_ * 512 + j * 128: p3_ * 512 + (j + 1) * 128],
                                          start=True, stop=True)
                        h_.inc("sm", 1)
                    inter_mm[h3_] = C["sm"]
                    for j in range(NJ):
                        bank = pf[4 + j]
                        v.wait("sm", inter_mm[h3_] - (NJ - 1 - j))
                        for nc_ in range(2):
                            sl = R2[:, (2 * j + nc_) * 4096 + h3_ * 128:
                                    (2 * j + nc_) * 4096 + (h3_ + 1) * 128]
                            v.scalar_tensor_tensor(sl, bank[:, nc_ * 128:(nc_ + 1) * 128],
                                                   qdc[:, h3_ * 2 + nc_: h3_ * 2 + nc_ + 1],
                                                   sl, ALU.mult, ALU.add).inc("sv", 1)
                if hh < NH:
                    qk_chunk(hh, 1)
                    if h1_ >= 0:
                        v.wait("sm", cmm[h1_])
                    for j in range(NJ):
                        for mc in range(2):
                            v.tensor_scalar(cvp[:, j * 256 + mc * 128: j * 256 + (mc + 1) * 128],
                                            R2[:, (2 * j + mc) * 4096 + hh * 128:
                                               (2 * j + mc) * 4096 + (hh + 1) * 128],
                                            kdc[:, 2 * hh + mc: 2 * hh + mc + 1], None,
                                            ALU.mult).inc("sv", 1)
                    cvp_cnt[hh] = C["sv"]
                    t.wait("sc", kevac[hh])
                    t.wait("sv")
                    for j in range(NJ):
                        for mc in range(2):
                            h_ = t.matmul(pf[4 + j][:, mc * 256:(mc + 1) * 256],
                                          evq[:, 1024 + j * 256 + mc * 128:
                                              1024 + j * 256 + (mc + 1) * 128],
                                          evq[:, j * 256:(j + 1) * 256],
                                          start=True, stop=True)
                    h_.inc("sm", 1)
                    scores_cnt[hh] = C["sm"]
            barrier(gx=["smh"])

            p3_smh = C.get("smh", 0)
            p3_ss = C.get("ss", 0)
            def g_evac(fo, tc):
                slot = tc % 4
                idx = fo * 8 + tc
                if idx >= 4:
                    s.wait("ss", p3_ss + 16 * (idx - 4 + 1))
                s.wait("smh", p3_smh + (fo + 1) * 16)
                s.activation(evq[:, slot * 512:(slot + 1) * 512], pf[tc][:],
                             AF.Sigmoid).inc("sc", 1)
                y.wait("sc")
                y.dma_start(gscr.ap()[tc * 128:(tc + 1) * 128, fo * 512:(fo + 1) * 512],
                            evq[:, slot * 512:(slot + 1) * 512]).inc("ss", 16)
            aform_real(wg_h, g_evac)
            barrier(gx=["smh"])

            g.dma_start(wst[:, 0:4096],
                        bass.AP(wa_h, 0, [[128, 128], [16384, 32], [1, 128]])).inc("sw", 16)
            wa_cnt = C["sw"]
            for kc in range(KC):
                for half in range(2):
                    h_ = t.matmul(pf[6 + half][0:8, :], wrb[:, kc * 8:(kc + 1) * 8],
                                  R1xn(kc, half), start=(kc == 0), stop=(kc == 31))
                    if kc == 31:
                        h_.inc("sm", 1)
            v.wait("sm")
            for half in range(2):
                v.tensor_copy(stage[0:8, half * 512:(half + 1) * 512],
                              pf[6 + half][0:8, :]).inc("sv", 1)
            lgev = C["sv"]
            t.wait("sv", lgev)
            for tc in range(TC):
                if tc >= 2:
                    t.wait("sv")
                h_ = t.transpose(pf[tc % 2][:, 0:8],
                                 stage[0:8, tc * 128:(tc + 1) * 128],
                                 idb[0:8, 0:8]).inc("sm", 1)
                v.wait("sm")
                v.tensor_copy(rwk[:, tc * 8:(tc + 1) * 8], pf[tc % 2][:, 0:8]).inc("sv", 1)
            lgcopy = C["sv"]
            def rep8(col0):
                return bass.AP(rwk, col0, [[288, 128], [1, 8], [0, 8]])
            v.wait("sv")
            for tc in range(TC):
                v.reduce_max(rwk[:, 256 + tc:257 + tc], rwk[:, tc * 8:(tc + 1) * 8],
                             axis=AX, negate=True).inc("sv", 1)
            v.wait("sv")
            v.tensor_tensor(rwk[:, 64:128], rwk[:, 0:64], rep8(256), ALU.add).inc("sv", 1)
            v.wait("sv")
            v.tensor_scalar(rwk[:, 128:192], rwk[:, 64:128], 0.0, None, ALU.is_ge).inc("sv", 1)
            s.wait("sv")
            s.activation(rwk[:, 192:256], rwk[:, 64:128], AF.Exp).inc("sc", 1)
            v.wait("sv")
            v.scalar_tensor_tensor(rwk[:, 0:64], rwk[:, 128:192], -1e30, rwk[:, 64:128],
                                   ALU.mult, ALU.add).inc("sv", 1)
            v.wait("sv")
            for tc in range(TC):
                v.reduce_max(rwk[:, 264 + tc:265 + tc], rwk[:, tc * 8:(tc + 1) * 8],
                             axis=AX, negate=True).inc("sv", 1)
            v.wait("sv")
            v.tensor_tensor(rwk[:, 0:64], rwk[:, 64:128], rep8(264), ALU.add).inc("sv", 1)
            v.wait("sv")
            v.tensor_scalar(rwk[:, 128:192], rwk[:, 0:64], 0.0, None, ALU.is_ge).inc("sv", 1)
            v.wait("sv")
            v.wait("sc")
            v.tensor_mul(rwk[:, 64:128], rwk[:, 128:192], rwk[:, 192:256]).inc("sv", 1)
            v.wait("sv")
            for tc in range(TC):
                v.reduce_sum(rwk[:, 272 + tc:273 + tc], rwk[:, 64 + tc * 8:64 + (tc + 1) * 8],
                             axis=AX).inc("sv", 1)
            v.wait("sv")
            v.reciprocal(rwk[:, 280:288], rwk[:, 272:280]).inc("sv", 1)
            v.wait("sv")
            v.scalar_tensor_tensor(cmbt[:, 0:64], rwk[:, 64:128], 2.0, rep8(280),
                                   ALU.mult, ALU.mult).inc("sv", 1)
            cmbdone = C["sv"]
            t.wait("sw", wa_cnt)
            t.wait("sv", lgcopy)
            hdone = [0] * TC
            for tc in range(TC):
                for kc in range(KC):
                    h_ = t.matmul(pf[tc][:, 0:128], R1t(kc, tc),
                                  wst[:, kc * 128:(kc + 1) * 128],
                                  start=(kc == 0), stop=(kc == 31))
                    if kc == 31:
                        h_.inc("sm", 1)
                hdone[tc] = C["sm"]
            g.wait("sm", hdone[TC - 1])
            g.dma_start(out_h.ap(), x_h.ap()).inc("ss", 16)
            xcopy = C["ss"]
            hw_cnt = [0] * TC
            tr_cnt = [0] * TC
            for tc in range(TC):
                v.wait("sm", hdone[tc])
                v.wait("sv", cmbdone)
                if tc >= 2:
                    v.wait("sm", tr_cnt[tc - 2])
                for e in range(8):
                    v.tensor_scalar(hwf[:, (tc % 2) * 128 + e * 16:
                                        (tc % 2) * 128 + (e + 1) * 16],
                                    pf[tc][:, e * 16:(e + 1) * 16],
                                    cmbt[:, tc * 8 + e: tc * 8 + e + 1], None,
                                    ALU.mult).inc("sv", 1)
                hw_cnt[tc] = C["sv"]
                t.wait("sv", hw_cnt[tc])
                t.transpose(pf[tc][:, 128:256], hwf[:, (tc % 2) * 128:(tc % 2 + 1) * 128],
                            idb[:]).inc("sm", 1)
                tr_cnt[tc] = C["sm"]
                v.wait("sm", tr_cnt[tc])
                v.tensor_copy(hwT[:, tc * 128:(tc + 1) * 128],
                              pf[tc][:, 128:256]).inc("sv", 1)
            barrier(gx=[])

            p5_sm0 = C.get("sm", 0)
            sqcnt = [0] * TC
            gldc = [0] * TC
            for half in range(2):
                s.activation(xbuf[:], R2[:, half * 2048:half * 2048 + 2048],
                             AF.Square, accum_out=rv1[:, half:half + 1]).inc("sc", 1)
            sqcnt[0] = C["sc"]
            y.dma_start(wst[:, 0:4096], gscr.ap()[0:128, :]).inc("sx", 16)
            gldc[0] = C["sx"]
            for tc in range(TC):
                prevev = C.get("sc", 0)
                v.wait("sc", sqcnt[tc])
                v.tensor_tensor(stt_[:, 8:9], rv1[:, 2 * (tc % 4):2 * (tc % 4) + 1],
                                rv1[:, 2 * (tc % 4) + 1:2 * (tc % 4) + 2], ALU.add).inc("sv", 1)
                v.wait("sv")
                v.tensor_scalar(stt_[:, 8:9], stt_[:, 8:9], 1.0 / DM, 1e-5,
                                ALU.mult, ALU.add).inc("sv", 1)
                v.wait("sv")
                v.reciprocal(stt_[:, 9:10], stt_[:, 8:9]).inc("sv", 1)
                s.wait("sv")
                s.activation(stt_[:, 10:11], stt_[:, 9:10], AF.Sqrt).inc("sc", 1)
                v.wait("sc")
                ghd = [0, 0]
                for half in range(2):
                    if tc > 0:
                        v.wait("sm", p5_sm0 + (tc - 1) * 32 + (half + 1) * 16)
                    v.wait("sv")
                    v.scalar_tensor_tensor(stage[:, half * 2048:(half + 1) * 2048],
                                           R2[:, tc * 4096 + half * 2048:
                                              tc * 4096 + (half + 1) * 2048],
                                           stt_[:, 10:11],
                                           nwb[:, half * 2048:(half + 1) * 2048],
                                           ALU.mult, ALU.mult).inc("sv", 1)
                    if half == 0:
                        v.wait("sx", gldc[tc])
                    v.wait("sv")
                    v.tensor_mul(stage[:, half * 2048:(half + 1) * 2048],
                                 stage[:, half * 2048:(half + 1) * 2048],
                                 wst[:, half * 2048:(half + 1) * 2048]).inc("sv", 1)
                    ghd[half] = C["sv"]
                if tc + 1 < TC:
                    y.wait("sv", ghd[1])
                    y.dma_start(wst[:, 0:4096],
                                gscr.ap()[(tc + 1) * 128:(tc + 2) * 128, :]).inc("sx", 16)
                    gldc[tc + 1] = C["sx"]
                if tc + 1 < TC:
                    for half in range(2):
                        s.wait("sv", ghd[0])
                        s.activation(xbuf[:],
                                     R2[:, (tc + 1) * 4096 + half * 2048:
                                        (tc + 1) * 4096 + (half + 1) * 2048],
                                     AF.Square,
                                     accum_out=rv1[:, 2 * ((tc + 1) % 4) + half:
                                                   2 * ((tc + 1) % 4) + half + 1]).inc("sc", 1)
                    sqcnt[tc + 1] = C["sc"]
                scb = C.get("sc", 0)
                for i in range(KC):
                    if i >= 8:
                        t.wait("sc", scb + i - 7)
                        t.wait("sv", ghd[1] if i >= 16 else ghd[0])
                    else:
                        t.wait("sc", prevev)
                        t.wait("sv", ghd[0])
                    t.transpose(pf[i % 8][:, 0:128],
                                stage[:, i * 128:(i + 1) * 128],
                                idb[:]).inc("sm", 1)
                    s.wait("sm")
                    s.activation(R2[:, tc * 4096 + i * 128: tc * 4096 + (i + 1) * 128],
                                 pf[i % 8][:, 0:128], AF.Copy).inc("sc", 1)
            barrier()

            p6_smh0 = C.get("smh", 0)
            p6_smb0 = C.get("smb_", 0)
            ev0 = C.get("sv", 0)
            so0 = C.get("so", 0)
            gi = 0
            bbld = [0] * 8
            pend = []
            def flush_out():
                for (fo_, tc_, evcnt) in pend:
                    g.wait("sv", evcnt)
                    if fo_ == 0 and tc_ == 0:
                        g.wait("ss", xcopy)
                    g.dma_start(out_h.ap()[tc_ * 128:(tc_ + 1) * 128,
                                           fo_ * 512:(fo_ + 1) * 512],
                                stage[:, tc_ * 512:(tc_ + 1) * 512],
                                accum_op=ALU.add).inc("so", 16)
                pend.clear()
            for fo in range(8):
                if fo >= 2:
                    g.wait("smb_", p6_smb0 + (fo - 1) * 8)
                g.dma_start(bst[:, (fo % 2) * 512:(fo % 2) * 512 + 512],
                            wbf_h.ap()[:, fo * 512:(fo + 1) * 512]).inc("sw", 16)
                bcnt = C["sw"]
                if fo >= 4:
                    g.wait("sv", ev0 + (fo - 4) * 8 + 8)
                g.dma_start(xbuf[:, (fo % 4) * 512:(fo % 4) * 512 + 512],
                            bass.AP(bb_h, fo * 512, [[0, 128], [1, 512]])).inc("sw", 16)
                bbld[fo] = C["sw"]
                for w_i, w_hh in enumerate((wo_h, wb_h)):
                    for kg in range(16):
                        slot = gi % 4
                        if gi >= 4:
                            g.wait("smh", p6_smh0 + gi - 3)
                        g.dma_start(
                            wst[:, slot * 1024:(slot + 1) * 1024],
                            bass.AP(w_hh, (2 * kg * 128) * DM + fo * 512,
                                    [[DM, 128], [128 * DM, 2], [1, 512]])).inc("sw", 16)
                        gcnt = C["sw"]
                        if w_i == 0 and kg == 4:
                            flush_out()
                        t.wait("sw", gcnt)
                        for j in range(2):
                            kc = 2 * kg + j
                            for tc in range(TC):
                                if w_i == 0 and kc == 0 and fo > 0:
                                    t.wait("sv", ev0 + (fo - 1) * 8 + tc + 1)
                                lhsT = (R2[:, tc * 4096 + kc * 128: tc * 4096 + (kc + 1) * 128]
                                        if w_i == 0 else R1t(kc, tc))
                                h_ = t.matmul(pf[tc][:], lhsT,
                                              wst[:, slot * 1024 + j * 512:
                                                  slot * 1024 + (j + 1) * 512],
                                              start=(w_i == 0 and kc == 0), stop=False)
                                if tc == TC - 1 and j == 1:
                                    h_.inc("smh", 1)
                        gi += 1
                t.wait("sw", bcnt)
                for tc in range(TC):
                    t.matmul(pf[tc][:], hwT[:, tc * 128:(tc + 1) * 128],
                             bst[:, (fo % 2) * 512:(fo % 2) * 512 + 512],
                             start=False, stop=True).inc("smb_", 1)
                for tc in range(TC):
                    v.wait("smb_")
                    v.wait("sw", bbld[fo])
                    if fo > 0:
                        v.wait("so", so0 + 16 * ((fo - 1) * 8 + tc + 1))
                    v.tensor_tensor(stage[:, tc * 512:(tc + 1) * 512], pf[tc][:],
                                    xbuf[:, (fo % 4) * 512:(fo % 4) * 512 + 512],
                                    ALU.add).inc("sv", 1)
                    pend.append((fo, tc, C["sv"]))
            flush_out()
            g.wait("so")
            y.wait("so")

        def run_engine(name):
            def fn(engine):
                env = _Env()
                env.cnt = {}
                env.sems = sems
                for nm in ("t", "v", "s", "g", "y"):
                    setattr(env, nm, _P(env, engine if nm == name else None))
                emit_all(env)
            return fn

        blk.tensor(run_engine("t"))
        blk.vector(run_engine("v"))
        blk.scalar(run_engine("s"))
        blk.gpsimd(run_engine("g"))
        blk.sync(run_engine("y"))

    return nc


def kernel(L0_attn_hidden_states, L0_attn_qkv_w, L0_attn_out_w, L0_attn_gate_w,
           L0_attn_norm_w, L0_ffn_W_base, L0_ffn_b_base, L0_ffn_W_router,
           L0_ffn_A, L0_ffn_B):
    x = np.asarray(L0_attn_hidden_states, dtype=np.float32)
    qkv_w = np.asarray(L0_attn_qkv_w, dtype=np.float32)
    out_w = np.asarray(L0_attn_out_w, dtype=np.float32)
    gate_w = np.asarray(L0_attn_gate_w, dtype=np.float32)
    norm_w = np.asarray(L0_attn_norm_w, dtype=np.float32)
    W_base = np.asarray(L0_ffn_W_base, dtype=np.float32)
    b_base = np.asarray(L0_ffn_b_base, dtype=np.float32)
    W_router = np.asarray(L0_ffn_W_router, dtype=np.float32)
    A = np.asarray(L0_ffn_A, dtype=np.float32)
    B = np.asarray(L0_ffn_B, dtype=np.float32)

    slope, qd64, kd64, dd64, bd64 = _decays()

    qk_w = np.empty((8192, DM), np.float32)
    v_w = np.empty((DM, DM), np.float32)
    for h in range(NH):
        qk_w[256 * h:256 * h + 128] = qkv_w[384 * h:384 * h + 128]
        qk_w[256 * h + 128:256 * h + 256] = qkv_w[384 * h + 128:384 * h + 256]
        v_w[128 * h:128 * (h + 1)] = qkv_w[384 * h + 256:384 * h + 384]
    wqk = np.ascontiguousarray(qk_w.reshape(64, 128, KC, 128).transpose(0, 3, 2, 1))
    wv = np.ascontiguousarray(v_w.T)
    wg = np.ascontiguousarray(gate_w.T)
    wo = np.ascontiguousarray(out_w.T)
    wb = np.ascontiguousarray(W_base.T)
    wr = np.ascontiguousarray(W_router.T.reshape(KC, 128, 8).transpose(1, 0, 2).reshape(128, 256))
    wa = np.ascontiguousarray(A.transpose(2, 0, 1).reshape(DM, 128))
    wbf = np.ascontiguousarray(B.transpose(0, 2, 1).reshape(128, DM))

    ddt = np.empty((NH, 2, 128, 256), np.float32)
    for h in range(NH):
        for mc in range(2):
            ddt[h, mc] = dd64[h][:, mc * 128:(mc + 1) * 128].T.astype(np.float32)
    kdt = np.empty((128, 64), np.float32)
    qdt = np.empty((128, 64), np.float32)
    for h in range(NH):
        for mc in range(2):
            kdt[:, h * 2 + mc] = kd64[h, mc * 128:(mc + 1) * 128]
            qdt[:, h * 2 + mc] = qd64[h, mc * 128:(mc + 1) * 128]
    idm = np.eye(128, dtype=np.float32)

    shared = dict(wqk=wqk, wv=wv, wg=wg, wo=wo, wb=wb, wr=wr, wa=wa, wbf=wbf,
                  nw=norm_w, bbv=b_base, ddt=ddt, kdt=kdt, qdt=qdt, idm=idm)
    in_maps = []
    for c in range(8):
        b, q = c // 4, c % 4
        cwm = np.zeros((NH, 4), np.float32)
        for r in range(q):
            cwm[:, r] = np.exp(-256.0 * slope * 4 * (q - 1 - r)).astype(np.float32)
        m = dict(shared)
        m["x"] = np.ascontiguousarray(x[b, q * TOK:(q + 1) * TOK, :])
        m["cw"] = cwm
        in_maps.append(m)

    nc = build_nc()
    res = run_bass_kernel_spmd(nc, in_maps, core_ids=list(range(8)))
    out = np.empty((2, 4096, DM), np.float32)
    for c in range(8):
        b, q = c // 4, c % 4
        out[b, q * TOK:(q + 1) * TOK, :] = res.results[c]["out"]
    return out


if __name__ == "__main__":
    import reference as _r
    inp = _r.setup_inputs()
    o = kernel(**{k: np.asarray(v) for k, v in inp.items()})
    ref = np.asarray(_r.reference(**inp))
    err = np.mean((o - ref) ** 2) / np.mean(ref ** 2)
    print("resid var ratio", err)


# version 14
# speedup vs baseline: 1.0103x; 1.0044x over previous
import numpy as np
import ml_dtypes
from contextlib import ExitStack
import concourse.bass as bass
import concourse.mybir as mybir
from concourse.bass_utils import run_bass_kernel_spmd

BF = ml_dtypes.bfloat16
F32, BF16 = mybir.dt.float32, mybir.dt.bfloat16
AF = mybir.ActivationFunctionType
ALU = mybir.AluOpType
AX = mybir.AxisListType.X

NH, HD, BLKSZ = 32, 128, 256
TOK = 1024
TC = 8
KC = 32
NJ = 4
DM = 4096


def _decays():
    base = np.float64(2.0) ** np.float64(-0.25)
    exponent = np.arange(NH, dtype=np.float64) + 1.0
    factor = np.float64(1.0) - np.float64(1.0) / (31 + 1e-5) + np.float64(1e-5)
    slope = base ** exponent * factor
    r = np.arange(BLKSZ, dtype=np.float64) + 1.0
    qd = np.exp(-slope[:, None] * r[None, :])
    kd = np.exp(-slope[:, None] * (BLKSZ - r)[None, :])
    diff = r[:, None] - r[None, :]
    dd = np.where(diff >= 0, np.exp(-slope[:, None, None] * diff[None]), 0.0)
    bd = np.exp(-slope * BLKSZ)
    return slope, qd, kd, dd, bd


class _H:
    __slots__ = ("env", "ins")
    def __init__(self, env, ins):
        self.env, self.ins = env, ins
    def inc(self, semname, k):
        self.env.cnt[semname] = self.env.cnt.get(semname, 0) + k
        if self.ins is not None:
            self.ins.then_inc(self.env.sems[semname], k)
        return self


class _P:
    def __init__(self, env, real):
        self._env, self._real = env, real
    def __getattr__(self, name):
        env, real = self._env, self._real
        def call(*a, **k):
            ins = getattr(real, name)(*a, **k) if real is not None else None
            return _H(env, ins)
        return call
    def wait(self, semname, val=None):
        v = self._env.cnt.get(semname, 0) if val is None else val
        if v <= 0:
            return
        if self._real is not None:
            self._real.wait_ge(self._env.sems[semname], v)


class _Env:
    pass


def build_nc():
    nc = bass.Bass("TRN2", target_bir_lowering=False)

    x_h = nc.dram_tensor("x", [TOK, DM], F32, kind="ExternalInput")
    wqk_h = nc.dram_tensor("wqk", [64, 128, KC, 128], F32, kind="ExternalInput")
    wv_h = nc.dram_tensor("wv", [DM, DM], F32, kind="ExternalInput")
    wg_h = nc.dram_tensor("wg", [DM, DM], F32, kind="ExternalInput")
    wo_h = nc.dram_tensor("wo", [DM, DM], F32, kind="ExternalInput")
    wb_h = nc.dram_tensor("wb", [DM, DM], F32, kind="ExternalInput")
    wr_h = nc.dram_tensor("wr", [128, 256], F32, kind="ExternalInput")
    wa_h = nc.dram_tensor("wa", [DM, 128], F32, kind="ExternalInput")
    wbf_h = nc.dram_tensor("wbf", [128, DM], F32, kind="ExternalInput")
    nw_h = nc.dram_tensor("nw", [DM], F32, kind="ExternalInput")
    bb_h = nc.dram_tensor("bbv", [DM], F32, kind="ExternalInput")
    ddt_h = nc.dram_tensor("ddt", [NH, 2, 128, 256], F32, kind="ExternalInput")
    kdt_h = nc.dram_tensor("kdt", [128, 64], F32, kind="ExternalInput")
    qdt_h = nc.dram_tensor("qdt", [128, 64], F32, kind="ExternalInput")
    cw_h = nc.dram_tensor("cw", [NH, 4], F32, kind="ExternalInput")
    idm_h = nc.dram_tensor("idm", [128, 128], F32, kind="ExternalInput")
    out_h = nc.dram_tensor("out", [TOK, DM], F32, kind="ExternalOutput")
    qkscr = nc.dram_tensor("qkscr", [64, 128, 1024], BF16)
    gscr = nc.dram_tensor("gscr", [TOK, DM], BF16)
    cl_h = [nc.dram_tensor(f"cl{h}", [128, 128], F32) for h in range(NH)]
    ca_h = [nc.dram_tensor(f"ca{h}", [4, 128, 128], F32) for h in range(NH)]

    _, _, _, _, bd64 = _decays()
    bdf = [float(np.float32(v)) for v in bd64]

    with ExitStack() as ctx:
        sb = lambda n, s, d: ctx.enter_context(nc.sbuf_tensor(n, s, d))
        psf = lambda n: ctx.enter_context(nc.psum_tensor(n, [128, 512], F32))
        R1 = sb("R1", [128, 32768], BF16)
        R2 = sb("R2", [128, 32768], BF16)
        xbuf = sb("xbuf", [128, 2048], F32)
        stage = sb("stage", [128, 4096], F32)
        wst = sb("wst", [128, 4096], BF16)
        evq = sb("evq", [128, 2048], BF16)
        nwb = sb("nwb", [128, 4096], BF16)
        qh2 = sb("qh2", [128, 2048], BF16)
        ddb = sb("ddb", [128, 512], F32)
        smb = sb("smb", [128, 2048], BF16)
        ckt = sb("ckt", [128, 1024], BF16)
        cvp = sb("cvp", [128, 1024], BF16)
        cbuf = sb("cbuf", [128, 1024], BF16)
        sfall = sb("sfall", [128, 512], F32)
        sfa = sb("sfa", [128, 128], F32)
        sba = sb("sba", [128, 1024], BF16)
        lca = sb("lca", [128, 128], F32)
        lcb = sb("lcb", [128, 128], F32)
        kdc = sb("kdc", [128, 64], F32)
        qdc = sb("qdc", [128, 64], F32)
        cwb = sb("cwb", [128, 128], F32)
        crd = sb("crd", [128, 1024], F32)
        cmbt = sb("cmbt", [128, 64], F32)
        hwf = sb("hwf", [128, 256], F32)
        hwT = sb("hwT", [128, 1024], BF16)
        idb = sb("idb", [128, 128], F32)
        stt_ = sb("stats", [128, 16], F32)
        rwk = sb("rwk", [128, 288], F32)
        rv1 = sb("rv1", [128, 8], F32)
        bst = sb("bst", [128, 1024], BF16)
        wrb = sb("wrb", [128, 256], BF16)
        pf = [psf(f"pf{i}") for i in range(8)]

        semnames = ["sx", "sw", "sp", "sd", "ss", "sm", "smh", "smb_", "sv",
                    "sc", "scc", "so", "st_", "sxc"]
        sems = {n: ctx.enter_context(nc.semaphore(n)) for n in semnames}
        blk = ctx.enter_context(nc.Block())

        def R1xn(kc, half):
            return R1[:, kc * 1024 + half * 512: kc * 1024 + (half + 1) * 512]

        def R1t(kc, tc):
            return R1[:, kc * 1024 + tc * 128: kc * 1024 + (tc + 1) * 128]

        def emit_all(env):
            t, v, s, g, y = env.t, env.v, env.s, env.g, env.y
            C = env.cnt

            def barrier(gx=None):
                for e in (t, v, s, g, y):
                    for sn in semnames:
                        if sn in ("scc", "sxc") or C.get(sn, 0) <= 0:
                            continue
                        if e is g and gx is not None and sn not in gx:
                            continue
                        e.wait(sn)

            g.dma_start(idb[:], idm_h.ap()).inc("sp", 16)
            g.dma_start(nwb[:], bass.AP(nw_h, 0, [[0, 128], [1, 4096]])).inc("sp", 16)
            g.dma_start(kdc[:], kdt_h.ap()).inc("sp", 16)
            g.dma_start(qdc[:], qdt_h.ap()).inc("sp", 16)
            g.dma_start(cwb[:], bass.AP(cw_h, 0, [[0, 128], [1, 128]])).inc("sp", 16)
            g.dma_start(wrb[:], wr_h.ap()).inc("sp", 16)
            for e in (t, v, s):
                e.wait("sp")

            stage_free = 0
            for tc in range(TC):
                half_sx = [0, 0]
                for half in range(2):
                    if half == 0:
                        y.wait("sv", stage_free)
                    y.dma_start(stage[:, half * 2048:(half + 1) * 2048],
                                x_h.ap()[tc * 128:(tc + 1) * 128,
                                         half * 2048:(half + 1) * 2048]).inc("sx", 16)
                    half_sx[half] = C["sx"]
                    v.wait("sx", half_sx[half])
                    v.reduce_sum(stt_[:, 12 + half:13 + half],
                                 stage[:, half * 2048:(half + 1) * 2048], axis=AX).inc("sv", 1)
                    s.wait("sx", half_sx[half])
                    s.activation(R2[:, 0:2048], stage[:, half * 2048:(half + 1) * 2048],
                                 AF.Square, accum_out=stt_[:, 14 + half:15 + half]).inc("sc", 1)
                v.wait("sv")
                v.tensor_tensor(stt_[:, 0:1], stt_[:, 12:13], stt_[:, 13:14], ALU.add).inc("sv", 1)
                v.wait("sv")
                v.tensor_scalar(stt_[:, 1:2], stt_[:, 0:1], -1.0 / DM, None, ALU.mult).inc("sv", 1)
                v.wait("sv")
                v.tensor_mul(stt_[:, 2:3], stt_[:, 1:2], stt_[:, 1:2]).inc("sv", 1)
                v.wait("sc")
                v.wait("sv")
                v.tensor_tensor(stt_[:, 3:4], stt_[:, 14:15], stt_[:, 15:16], ALU.add).inc("sv", 1)
                v.wait("sv")
                v.scalar_tensor_tensor(stt_[:, 4:5], stt_[:, 3:4], 1.0 / DM, stt_[:, 2:3],
                                       ALU.mult, ALU.subtract).inc("sv", 1)
                v.wait("sv")
                v.tensor_scalar(stt_[:, 4:5], stt_[:, 4:5], 1e-5, None, ALU.add).inc("sv", 1)
                v.wait("sv")
                v.reciprocal(stt_[:, 5:6], stt_[:, 4:5]).inc("sv", 1)
                s.wait("sv")
                s.activation(stt_[:, 6:7], stt_[:, 5:6], AF.Sqrt).inc("sc", 1)
                for half in range(2):
                    v.wait("sc")
                    if tc > 0 or half > 0:
                        v.wait("sm")
                    v.tensor_scalar(xbuf[:], stage[:, half * 2048:(half + 1) * 2048],
                                    stt_[:, 1:2], stt_[:, 6:7],
                                    ALU.add, ALU.mult).inc("sv", 1)
                    xnh = C["sv"]
                    if half == 1:
                        stage_free = xnh
                    for kcl in range(16):
                        kc = half * 16 + kcl
                        if kcl >= 8:
                            t.wait("sv", xnh + kcl - 7)
                        else:
                            t.wait("sv", xnh)
                        t.transpose(pf[kc % 8][:, 0:128], xbuf[:, kcl * 128:(kcl + 1) * 128],
                                    idb[:]).inc("sm", 1)
                        v.wait("sm")
                        v.tensor_copy(R1[:, kc * 1024 + tc * 128: kc * 1024 + (tc + 1) * 128],
                                      pf[kc % 8][:, 0:128]).inc("sv", 1)
            barrier(gx=[])

            def aform_real(w_hh, evac_cb):
                base_smh = C.get("smh", 0)
                base_sw = C.get("sw", 0)
                base_sc = C.get("sc", 0)
                gi = 0
                for fo in range(8):
                    for kg in range(16):
                        if gi >= 4:
                            g.wait("smh", base_smh + gi - 3)
                        g.dma_start(
                            wst[:, (gi % 4) * 1024:(gi % 4) * 1024 + 1024],
                            bass.AP(w_hh, (2 * kg * 128) * DM + fo * 512,
                                    [[DM, 128], [128 * DM, 2], [1, 512]])).inc("sw", 16)
                        t.wait("sw", base_sw + 16 * (gi + 1))
                        for j in range(2):
                            kc = 2 * kg + j
                            for tc in range(TC):
                                if kc == 0 and fo > 0:
                                    t.wait("sc", base_sc + (fo - 1) * 8 + tc + 1)
                                h_ = t.matmul(pf[tc][:], R1t(kc, tc),
                                              wst[:, (gi % 4) * 1024 + j * 512:
                                                  (gi % 4) * 1024 + (j + 1) * 512],
                                              start=(kc == 0), stop=(kc == 31))
                                if tc == TC - 1 and j == 1:
                                    h_.inc("smh", 1)
                        gi += 1
                    for tc in range(TC):
                        evac_cb(fo, tc)

            p1_smh = C.get("smh", 0)
            def v_evac(fo, tc):
                s.wait("smh", p1_smh + (fo + 1) * 16)
                s.activation(R2[:, tc * 4096 + fo * 512: tc * 4096 + (fo + 1) * 512],
                             pf[tc][:], AF.Silu).inc("sc", 1)
            aform_real(wv_h, v_evac)
            barrier(gx=["smh"])

            p2_smh0 = C.get("smh", 0)
            p2_sw0 = C.get("sw", 0)
            ST = {"hi": 0}
            Z = -1
            dd_cnt = [0] * NH; kspill = [0] * NH; scores_cnt = [0] * NH
            mask_cnt = [0] * NH; cmm = [0] * NH; intra_ev = [0] * NH
            horner = [0] * NH; carry_ss = [0] * NH; crd_cnt = [0] * NH
            qld = [0] * NH; cast_cnt = [0] * NH; inter_mm = [0] * NH
            schain = [0] * NH; qevac = [0] * NH; kevac = [0] * NH
            ck_cnt = [0] * NH; cvp_cnt = [0] * NH

            def qk_chunk(h, piece):
                c = 2 * h + piece
                pi = 2 * piece
                for half in range(2):
                    hi = ST["hi"]
                    if hi >= 2:
                        g.wait("smh", p2_smh0 + hi - 1)
                    g.dma_start(
                        wst[:, (hi % 2) * 2048:(hi % 2) * 2048 + 2048],
                        bass.AP(wqk_h, c * 128 * DM + half * 2048,
                                [[DM, 128], [1, 2048]])).inc("sw", 16)
                    t.wait("sw", p2_sw0 + 16 * (hi + 1))
                    if half == 0 and h > 0:
                        t.wait("sc", (qevac if piece == 0 else kevac)[h - 1])
                    for kcl in range(16):
                        kc = half * 16 + kcl
                        lhsT = wst[:, (hi % 2) * 2048 + kcl * 128:
                                   (hi % 2) * 2048 + (kcl + 1) * 128]
                        t.matmul(pf[pi][:], lhsT, R1xn(kc, 0),
                                 start=(kc == 0), stop=(kc == 31))
                        h1 = t.matmul(pf[pi + 1][:], lhsT, R1xn(kc, 1),
                                      start=(kc == 0), stop=(kc == 31))
                        if kcl == 15:
                            h1.inc("smh", 1)
                    ST["hi"] += 1
                s.wait("smh")
                for bhalf in range(2):
                    s.activation(evq[:, piece * 1024 + bhalf * 512:
                                     piece * 1024 + (bhalf + 1) * 512],
                                 pf[pi + bhalf][:], AF.Silu).inc("sc", 1)
                if piece == 0:
                    qevac[h] = C["sc"]
                else:
                    kevac[h] = C["sc"]
                y.wait("sc")
                y.dma_start(qkscr.ap()[c, :, :],
                            evq[:, piece * 1024:(piece + 1) * 1024]).inc("ss", 16)
                if piece == 1:
                    kspill[h] = C["ss"]

            for hh in range(NH + 3):
                h1_ = hh - 1
                h2_ = hh - 2
                h3_ = hh - 3
                if 0 <= h1_ < NH:
                    v.wait("sm", scores_cnt[h1_])
                    v.wait("sd", dd_cnt[h1_])
                    for j in range(NJ):
                        for mc in range(2):
                            v.tensor_mul(smb[:, j * 512 + mc * 256: j * 512 + (mc + 1) * 256],
                                         pf[4 + j][:, mc * 256:(mc + 1) * 256],
                                         ddb[:, mc * 256:(mc + 1) * 256]).inc("sv", 1)
                    mask_cnt[h1_] = C["sv"]
                if hh < NH:
                    if h1_ >= 0:
                        y.wait("sv", mask_cnt[h1_])
                    y.dma_start(ddb[:], bass.AP(ddt_h, hh * 65536,
                                                [[256, 128], [32768, 2], [1, 256]])).inc("sd", 16)
                    dd_cnt[hh] = C["sd"]
                if 0 <= h2_ < NH:
                    g.wait("ss", carry_ss[h2_])
                    g.collective_compute("AllGather", ALU.bypass,
                                         replica_groups=[[0, 1, 2, 3], [4, 5, 6, 7]],
                                         ins=[cl_h[h2_].ap()],
                                         outs=[ca_h[h2_].ap()]).inc("scc", 1)
                    if h2_ >= 2:
                        y.wait("sm", inter_mm[h2_ - 2])
                    y.dma_start(qh2[:, (h2_ % 2) * 1024:(h2_ % 2 + 1) * 1024],
                                qkscr.ap()[2 * h2_, :, :]).inc("sx", 16)
                    qld[h2_] = C["sx"]
                if 0 <= h3_ < NH:
                    s.wait("scc", h3_ + 1)
                    if h3_ >= 2:
                        s.wait("sv", schain[h3_ - 2])
                    s.dma_start(crd[:, (h3_ % 2) * 512:(h3_ % 2) * 512 + 512],
                                bass.AP(ca_h[h3_], 0,
                                        [[128, 128], [16384, 4], [1, 128]])).inc("st_", 16)
                    crd_cnt[h3_] = C["st_"]
                if 0 <= h1_ < NH:
                    s.wait("ss", kspill[h1_])
                    if h1_ >= 1:
                        s.wait("sm", cmm[h1_ - 1])
                    for j in range(NJ):
                        for mc in range(2):
                            s.dma_start_transpose(
                                ckt[:, j * 256 + mc * 128: j * 256 + (mc + 1) * 128],
                                qkscr.ap()[2 * h1_ + 1, :,
                                           j * 256 + mc * 128: j * 256 + (mc + 1) * 128]
                            ).inc("st_", 16)
                    ck_cnt[h1_] = C["st_"]
                if 0 <= h3_ < NH:
                    p3_ = h3_ % 2
                    v.wait("st_", crd_cnt[h3_])
                    v.tensor_scalar(sfa[:], crd[:, p3_ * 512: p3_ * 512 + 128],
                                    cwb[:, h3_ * 4: h3_ * 4 + 1], None, ALU.mult).inc("sv", 1)
                    v.wait("sv")
                    v.scalar_tensor_tensor(sfall[:, 0:128],
                                           crd[:, p3_ * 512 + 128: p3_ * 512 + 256],
                                           cwb[:, h3_ * 4 + 1: h3_ * 4 + 2], sfa[:],
                                           ALU.mult, ALU.add).inc("sv", 1)
                    v.wait("sv")
                    v.scalar_tensor_tensor(sfa[:],
                                           crd[:, p3_ * 512 + 256: p3_ * 512 + 384],
                                           cwb[:, h3_ * 4 + 2: h3_ * 4 + 3],
                                           sfall[:, 0:128], ALU.mult, ALU.add).inc("sv", 1)
                    v.wait("sv")
                    v.scalar_tensor_tensor(sfall[:, 0:128],
                                           crd[:, p3_ * 512 + 384: p3_ * 512 + 512],
                                           cwb[:, h3_ * 4 + 3: h3_ * 4 + 4], sfa[:],
                                           ALU.mult, ALU.add).inc("sv", 1)
                    for j in range(1, NJ):
                        v.wait("sv")
                        v.scalar_tensor_tensor(sfall[:, j * 128:(j + 1) * 128],
                                               sfall[:, (j - 1) * 128: j * 128], bdf[h3_],
                                               cbuf[:, p3_ * 512 + (j - 1) * 128:
                                                    p3_ * 512 + j * 128],
                                               ALU.mult, ALU.add).inc("sv", 1)
                    schain[h3_] = C["sv"]
                    s.wait("sv", schain[h3_])
                    if h3_ >= 2:
                        s.wait("sm", inter_mm[h3_ - 2])
                    s.activation(sba[:, p3_ * 512:(p3_ + 1) * 512], sfall[:],
                                 AF.Copy).inc("sc", 1)
                    cast_cnt[h3_] = C["sc"]
                if hh < NH:
                    qk_chunk(hh, 0)
                if 0 <= h1_ < NH:
                    t.wait("sv", mask_cnt[h1_])
                    t.wait("st_", ck_cnt[h1_])
                    for j in range(NJ):
                        bank = pf[4 + j]
                        for nc_ in range(2):
                            for mc in range(2):
                                t.matmul(bank[:, nc_ * 128:(nc_ + 1) * 128],
                                         smb[:, j * 512 + mc * 256 + nc_ * 128:
                                             j * 512 + mc * 256 + (nc_ + 1) * 128],
                                         R2[:, (2 * j + mc) * 4096 + h1_ * 128:
                                            (2 * j + mc) * 4096 + (h1_ + 1) * 128],
                                         start=(mc == 0), stop=(mc == 1))
                        for mc in range(2):
                            h_ = t.matmul(bank[:, 256:384],
                                          ckt[:, j * 256 + mc * 128: j * 256 + (mc + 1) * 128],
                                          cvp[:, j * 256 + mc * 128: j * 256 + (mc + 1) * 128],
                                          start=(mc == 0), stop=(mc == 1))
                        h_.inc("sm", 1)
                    cmm[h1_] = C["sm"]
                    p1_ = h1_ % 2
                    for j in range(NJ):
                        bank = pf[4 + j]
                        v.wait("sm", cmm[h1_] - (NJ - 1 - j))
                        for nc_ in range(2):
                            v.tensor_copy(R2[:, (2 * j + nc_) * 4096 + h1_ * 128:
                                             (2 * j + nc_) * 4096 + (h1_ + 1) * 128],
                                          bank[:, nc_ * 128:(nc_ + 1) * 128]).inc("sv", 1)
                        v.tensor_copy(cbuf[:, p1_ * 512 + j * 128: p1_ * 512 + (j + 1) * 128],
                                      bank[:, 256:384]).inc("sv", 1)
                        if j == 0:
                            v.tensor_copy(lca[:], bank[:, 256:384]).inc("sv", 1)
                        else:
                            lsrc, ldst = (lca, lcb) if j % 2 == 1 else (lcb, lca)
                            if j == 1 and h1_ >= 1:
                                v.wait("ss", carry_ss[h1_ - 1])
                            v.wait("sv")
                            v.scalar_tensor_tensor(ldst[:], lsrc[:], bdf[h1_],
                                                   bank[:, 256:384],
                                                   ALU.mult, ALU.add).inc("sv", 1)
                    intra_ev[h1_] = C["sv"]
                    horner[h1_] = C["sv"]
                    y.wait("sv", horner[h1_])
                    y.dma_start(cl_h[h1_].ap(), lcb[:]).inc("ss", 16)
                    carry_ss[h1_] = C["ss"]
                if 0 <= h3_ < NH:
                    p3_ = h3_ % 2
                    t.wait("sc", cast_cnt[h3_])
                    t.wait("sx", qld[h3_])
                    if 0 <= h1_ < NH:
                        t.wait("sv", intra_ev[h1_])
                    for j in range(NJ):
                        bank = pf[4 + j]
                        for nc_ in range(2):
                            h_ = t.matmul(bank[:, nc_ * 128:(nc_ + 1) * 128],
                                          qh2[:, p3_ * 1024 + j * 256 + nc_ * 128:
                                              p3_ * 1024 + j * 256 + (nc_ + 1) * 128],
                                          sba[:, p3_ * 512 + j * 128: p3_ * 512 + (j + 1) * 128],
                                          start=True, stop=True)
                        h_.inc("sm", 1)
                    inter_mm[h3_] = C["sm"]
                    for j in range(NJ):
                        bank = pf[4 + j]
                        v.wait("sm", inter_mm[h3_] - (NJ - 1 - j))
                        for nc_ in range(2):
                            sl = R2[:, (2 * j + nc_) * 4096 + h3_ * 128:
                                    (2 * j + nc_) * 4096 + (h3_ + 1) * 128]
                            v.scalar_tensor_tensor(sl, bank[:, nc_ * 128:(nc_ + 1) * 128],
                                                   qdc[:, h3_ * 2 + nc_: h3_ * 2 + nc_ + 1],
                                                   sl, ALU.mult, ALU.add).inc("sv", 1)
                if hh < NH:
                    qk_chunk(hh, 1)
                    if h1_ >= 0:
                        v.wait("sm", cmm[h1_])
                    for j in range(NJ):
                        for mc in range(2):
                            v.tensor_scalar(cvp[:, j * 256 + mc * 128: j * 256 + (mc + 1) * 128],
                                            R2[:, (2 * j + mc) * 4096 + hh * 128:
                                               (2 * j + mc) * 4096 + (hh + 1) * 128],
                                            kdc[:, 2 * hh + mc: 2 * hh + mc + 1], None,
                                            ALU.mult).inc("sv", 1)
                    cvp_cnt[hh] = C["sv"]
                    t.wait("sc", kevac[hh])
                    t.wait("sv")
                    for j in range(NJ):
                        for mc in range(2):
                            h_ = t.matmul(pf[4 + j][:, mc * 256:(mc + 1) * 256],
                                          evq[:, 1024 + j * 256 + mc * 128:
                                              1024 + j * 256 + (mc + 1) * 128],
                                          evq[:, j * 256:(j + 1) * 256],
                                          start=True, stop=True)
                    h_.inc("sm", 1)
                    scores_cnt[hh] = C["sm"]
            barrier(gx=["smh"])

            p3_smh = C.get("smh", 0)
            p3_ss = C.get("ss", 0)
            def g_evac(fo, tc):
                slot = tc % 4
                idx = fo * 8 + tc
                if idx >= 4:
                    s.wait("ss", p3_ss + 16 * (idx - 4 + 1))
                s.wait("smh", p3_smh + (fo + 1) * 16)
                s.activation(evq[:, slot * 512:(slot + 1) * 512], pf[tc][:],
                             AF.Sigmoid).inc("sc", 1)
                y.wait("sc")
                y.dma_start(gscr.ap()[tc * 128:(tc + 1) * 128, fo * 512:(fo + 1) * 512],
                            evq[:, slot * 512:(slot + 1) * 512]).inc("ss", 16)
            aform_real(wg_h, g_evac)
            barrier(gx=["smh"])

            g.dma_start(wst[:, 0:4096],
                        bass.AP(wa_h, 0, [[128, 128], [16384, 32], [1, 128]])).inc("sw", 16)
            wa_cnt = C["sw"]
            for kc in range(KC):
                for half in range(2):
                    h_ = t.matmul(pf[6 + half][0:8, :], wrb[:, kc * 8:(kc + 1) * 8],
                                  R1xn(kc, half), start=(kc == 0), stop=(kc == 31))
                    if kc == 31:
                        h_.inc("sm", 1)
            v.wait("sm")
            for half in range(2):
                v.tensor_copy(stage[0:8, half * 512:(half + 1) * 512],
                              pf[6 + half][0:8, :]).inc("sv", 1)
            lgev = C["sv"]
            t.wait("sv", lgev)
            for tc in range(TC):
                if tc >= 2:
                    t.wait("sv")
                h_ = t.transpose(pf[tc % 2][:, 0:8],
                                 stage[0:8, tc * 128:(tc + 1) * 128],
                                 idb[0:8, 0:8]).inc("sm", 1)
                v.wait("sm")
                v.tensor_copy(rwk[:, tc * 8:(tc + 1) * 8], pf[tc % 2][:, 0:8]).inc("sv", 1)
            lgcopy = C["sv"]
            def rep8(col0):
                return bass.AP(rwk, col0, [[288, 128], [1, 8], [0, 8]])
            v.wait("sv")
            for tc in range(TC):
                v.reduce_max(rwk[:, 256 + tc:257 + tc], rwk[:, tc * 8:(tc + 1) * 8],
                             axis=AX, negate=True).inc("sv", 1)
            v.wait("sv")
            v.tensor_tensor(rwk[:, 64:128], rwk[:, 0:64], rep8(256), ALU.add).inc("sv", 1)
            v.wait("sv")
            v.tensor_scalar(rwk[:, 128:192], rwk[:, 64:128], 0.0, None, ALU.is_ge).inc("sv", 1)
            s.wait("sv")
            s.activation(rwk[:, 192:256], rwk[:, 64:128], AF.Exp).inc("sc", 1)
            v.wait("sv")
            v.scalar_tensor_tensor(rwk[:, 0:64], rwk[:, 128:192], -1e30, rwk[:, 64:128],
                                   ALU.mult, ALU.add).inc("sv", 1)
            v.wait("sv")
            for tc in range(TC):
                v.reduce_max(rwk[:, 264 + tc:265 + tc], rwk[:, tc * 8:(tc + 1) * 8],
                             axis=AX, negate=True).inc("sv", 1)
            v.wait("sv")
            v.tensor_tensor(rwk[:, 0:64], rwk[:, 64:128], rep8(264), ALU.add).inc("sv", 1)
            v.wait("sv")
            v.tensor_scalar(rwk[:, 128:192], rwk[:, 0:64], 0.0, None, ALU.is_ge).inc("sv", 1)
            v.wait("sv")
            v.wait("sc")
            v.tensor_mul(rwk[:, 64:128], rwk[:, 128:192], rwk[:, 192:256]).inc("sv", 1)
            v.wait("sv")
            for tc in range(TC):
                v.reduce_sum(rwk[:, 272 + tc:273 + tc], rwk[:, 64 + tc * 8:64 + (tc + 1) * 8],
                             axis=AX).inc("sv", 1)
            v.wait("sv")
            v.reciprocal(rwk[:, 280:288], rwk[:, 272:280]).inc("sv", 1)
            v.wait("sv")
            v.scalar_tensor_tensor(cmbt[:, 0:64], rwk[:, 64:128], 2.0, rep8(280),
                                   ALU.mult, ALU.mult).inc("sv", 1)
            cmbdone = C["sv"]
            t.wait("sw", wa_cnt)
            t.wait("sv", lgcopy)
            hdone = [0] * TC
            for tc in range(TC):
                for kc in range(KC):
                    h_ = t.matmul(pf[tc][:, 0:128], R1t(kc, tc),
                                  wst[:, kc * 128:(kc + 1) * 128],
                                  start=(kc == 0), stop=(kc == 31))
                    if kc == 31:
                        h_.inc("sm", 1)
                hdone[tc] = C["sm"]
            g.wait("sm", hdone[TC - 1])
            g.dma_start(out_h.ap(), x_h.ap()).inc("sxc", 16)
            hw_cnt = [0] * TC
            tr_cnt = [0] * TC
            for tc in range(TC):
                v.wait("sm", hdone[tc])
                v.wait("sv", cmbdone)
                if tc >= 2:
                    v.wait("sm", tr_cnt[tc - 2])
                for e in range(8):
                    v.tensor_scalar(hwf[:, (tc % 2) * 128 + e * 16:
                                        (tc % 2) * 128 + (e + 1) * 16],
                                    pf[tc][:, e * 16:(e + 1) * 16],
                                    cmbt[:, tc * 8 + e: tc * 8 + e + 1], None,
                                    ALU.mult).inc("sv", 1)
                hw_cnt[tc] = C["sv"]
                t.wait("sv", hw_cnt[tc])
                t.transpose(pf[tc][:, 128:256], hwf[:, (tc % 2) * 128:(tc % 2 + 1) * 128],
                            idb[:]).inc("sm", 1)
                tr_cnt[tc] = C["sm"]
                v.wait("sm", tr_cnt[tc])
                v.tensor_copy(hwT[:, tc * 128:(tc + 1) * 128],
                              pf[tc][:, 128:256]).inc("sv", 1)
            barrier(gx=[])

            p5_sm0 = C.get("sm", 0)
            sqcnt = [0] * TC
            gldc = [0] * TC
            for half in range(2):
                s.activation(xbuf[:], R2[:, half * 2048:half * 2048 + 2048],
                             AF.Square, accum_out=rv1[:, half:half + 1]).inc("sc", 1)
            sqcnt[0] = C["sc"]
            y.dma_start(wst[:, 0:4096], gscr.ap()[0:128, :]).inc("sx", 16)
            gldc[0] = C["sx"]
            for tc in range(TC):
                prevev = C.get("sc", 0)
                v.wait("sc", sqcnt[tc])
                v.tensor_tensor(stt_[:, 8:9], rv1[:, 2 * (tc % 4):2 * (tc % 4) + 1],
                                rv1[:, 2 * (tc % 4) + 1:2 * (tc % 4) + 2], ALU.add).inc("sv", 1)
                v.wait("sv")
                v.tensor_scalar(stt_[:, 8:9], stt_[:, 8:9], 1.0 / DM, 1e-5,
                                ALU.mult, ALU.add).inc("sv", 1)
                v.wait("sv")
                v.reciprocal(stt_[:, 9:10], stt_[:, 8:9]).inc("sv", 1)
                s.wait("sv")
                s.activation(stt_[:, 10:11], stt_[:, 9:10], AF.Sqrt).inc("sc", 1)
                v.wait("sc")
                ghd = [0, 0]
                for half in range(2):
                    if tc > 0:
                        v.wait("sm", p5_sm0 + (tc - 1) * 32 + (half + 1) * 16)
                    v.wait("sv")
                    v.scalar_tensor_tensor(stage[:, half * 2048:(half + 1) * 2048],
                                           R2[:, tc * 4096 + half * 2048:
                                              tc * 4096 + (half + 1) * 2048],
                                           stt_[:, 10:11],
                                           nwb[:, half * 2048:(half + 1) * 2048],
                                           ALU.mult, ALU.mult).inc("sv", 1)
                    if half == 0:
                        v.wait("sx", gldc[tc])
                    v.wait("sv")
                    v.tensor_mul(stage[:, half * 2048:(half + 1) * 2048],
                                 stage[:, half * 2048:(half + 1) * 2048],
                                 wst[:, half * 2048:(half + 1) * 2048]).inc("sv", 1)
                    ghd[half] = C["sv"]
                if tc + 1 < TC:
                    y.wait("sv", ghd[1])
                    y.dma_start(wst[:, 0:4096],
                                gscr.ap()[(tc + 1) * 128:(tc + 2) * 128, :]).inc("sx", 16)
                    gldc[tc + 1] = C["sx"]
                if tc + 1 < TC:
                    for half in range(2):
                        s.wait("sv", ghd[0])
                        s.activation(xbuf[:],
                                     R2[:, (tc + 1) * 4096 + half * 2048:
                                        (tc + 1) * 4096 + (half + 1) * 2048],
                                     AF.Square,
                                     accum_out=rv1[:, 2 * ((tc + 1) % 4) + half:
                                                   2 * ((tc + 1) % 4) + half + 1]).inc("sc", 1)
                    sqcnt[tc + 1] = C["sc"]
                scb = C.get("sc", 0)
                for i in range(KC):
                    if i >= 8:
                        t.wait("sc", scb + i - 7)
                        t.wait("sv", ghd[1] if i >= 16 else ghd[0])
                    else:
                        t.wait("sc", prevev)
                        t.wait("sv", ghd[0])
                    t.transpose(pf[i % 8][:, 0:128],
                                stage[:, i * 128:(i + 1) * 128],
                                idb[:]).inc("sm", 1)
                    s.wait("sm")
                    s.activation(R2[:, tc * 4096 + i * 128: tc * 4096 + (i + 1) * 128],
                                 pf[i % 8][:, 0:128], AF.Copy).inc("sc", 1)
            barrier()

            p6_smh0 = C.get("smh", 0)
            p6_smb0 = C.get("smb_", 0)
            ev0 = C.get("sv", 0)
            so0 = C.get("so", 0)
            gi = 0
            bbld = [0] * 8
            pend = []
            def flush_out():
                for (fo_, tc_, evcnt) in pend:
                    g.wait("sv", evcnt)
                    if fo_ == 0 and tc_ == 0:
                        g.wait("sxc", 16)
                    g.dma_start(out_h.ap()[tc_ * 128:(tc_ + 1) * 128,
                                           fo_ * 512:(fo_ + 1) * 512],
                                stage[:, tc_ * 512:(tc_ + 1) * 512],
                                accum_op=ALU.add).inc("so", 16)
                pend.clear()
            for fo in range(8):
                if fo >= 2:
                    g.wait("smb_", p6_smb0 + (fo - 1) * 8)
                g.dma_start(bst[:, (fo % 2) * 512:(fo % 2) * 512 + 512],
                            wbf_h.ap()[:, fo * 512:(fo + 1) * 512]).inc("sw", 16)
                bcnt = C["sw"]
                if fo >= 4:
                    g.wait("sv", ev0 + (fo - 4) * 8 + 8)
                g.dma_start(xbuf[:, (fo % 4) * 512:(fo % 4) * 512 + 512],
                            bass.AP(bb_h, fo * 512, [[0, 128], [1, 512]])).inc("sw", 16)
                bbld[fo] = C["sw"]
                for w_i, w_hh in enumerate((wo_h, wb_h)):
                    for kg in range(16):
                        slot = gi % 4
                        if gi >= 4:
                            g.wait("smh", p6_smh0 + gi - 3)
                        g.dma_start(
                            wst[:, slot * 1024:(slot + 1) * 1024],
                            bass.AP(w_hh, (2 * kg * 128) * DM + fo * 512,
                                    [[DM, 128], [128 * DM, 2], [1, 512]])).inc("sw", 16)
                        gcnt = C["sw"]
                        if w_i == 0 and kg == 4:
                            flush_out()
                        t.wait("sw", gcnt)
                        for j in range(2):
                            kc = 2 * kg + j
                            for tc in range(TC):
                                if w_i == 0 and kc == 0 and fo > 0:
                                    t.wait("sv", ev0 + (fo - 1) * 8 + tc + 1)
                                lhsT = (R2[:, tc * 4096 + kc * 128: tc * 4096 + (kc + 1) * 128]
                                        if w_i == 0 else R1t(kc, tc))
                                h_ = t.matmul(pf[tc][:], lhsT,
                                              wst[:, slot * 1024 + j * 512:
                                                  slot * 1024 + (j + 1) * 512],
                                              start=(w_i == 0 and kc == 0), stop=False)
                                if tc == TC - 1 and j == 1:
                                    h_.inc("smh", 1)
                        gi += 1
                t.wait("sw", bcnt)
                for tc in range(TC):
                    t.matmul(pf[tc][:], hwT[:, tc * 128:(tc + 1) * 128],
                             bst[:, (fo % 2) * 512:(fo % 2) * 512 + 512],
                             start=False, stop=True).inc("smb_", 1)
                for tc in range(TC):
                    v.wait("smb_")
                    v.wait("sw", bbld[fo])
                    if fo > 0:
                        v.wait("so", so0 + 16 * ((fo - 1) * 8 + tc + 1))
                    v.tensor_tensor(stage[:, tc * 512:(tc + 1) * 512], pf[tc][:],
                                    xbuf[:, (fo % 4) * 512:(fo % 4) * 512 + 512],
                                    ALU.add).inc("sv", 1)
                    pend.append((fo, tc, C["sv"]))
            flush_out()
            g.wait("so")
            y.wait("so")

        def run_engine(name):
            def fn(engine):
                env = _Env()
                env.cnt = {}
                env.sems = sems
                for nm in ("t", "v", "s", "g", "y"):
                    setattr(env, nm, _P(env, engine if nm == name else None))
                emit_all(env)
            return fn

        blk.tensor(run_engine("t"))
        blk.vector(run_engine("v"))
        blk.scalar(run_engine("s"))
        blk.gpsimd(run_engine("g"))
        blk.sync(run_engine("y"))

    return nc


def kernel(L0_attn_hidden_states, L0_attn_qkv_w, L0_attn_out_w, L0_attn_gate_w,
           L0_attn_norm_w, L0_ffn_W_base, L0_ffn_b_base, L0_ffn_W_router,
           L0_ffn_A, L0_ffn_B):
    x = np.asarray(L0_attn_hidden_states, dtype=np.float32)
    qkv_w = np.asarray(L0_attn_qkv_w, dtype=np.float32)
    out_w = np.asarray(L0_attn_out_w, dtype=np.float32)
    gate_w = np.asarray(L0_attn_gate_w, dtype=np.float32)
    norm_w = np.asarray(L0_attn_norm_w, dtype=np.float32)
    W_base = np.asarray(L0_ffn_W_base, dtype=np.float32)
    b_base = np.asarray(L0_ffn_b_base, dtype=np.float32)
    W_router = np.asarray(L0_ffn_W_router, dtype=np.float32)
    A = np.asarray(L0_ffn_A, dtype=np.float32)
    B = np.asarray(L0_ffn_B, dtype=np.float32)

    slope, qd64, kd64, dd64, bd64 = _decays()

    qk_w = np.empty((8192, DM), np.float32)
    v_w = np.empty((DM, DM), np.float32)
    for h in range(NH):
        qk_w[256 * h:256 * h + 128] = qkv_w[384 * h:384 * h + 128]
        qk_w[256 * h + 128:256 * h + 256] = qkv_w[384 * h + 128:384 * h + 256]
        v_w[128 * h:128 * (h + 1)] = qkv_w[384 * h + 256:384 * h + 384]
    wqk = np.ascontiguousarray(qk_w.reshape(64, 128, KC, 128).transpose(0, 3, 2, 1))
    wv = np.ascontiguousarray(v_w.T)
    wg = np.ascontiguousarray(gate_w.T)
    wo = np.ascontiguousarray(out_w.T)
    wb = np.ascontiguousarray(W_base.T)
    wr = np.ascontiguousarray(W_router.T.reshape(KC, 128, 8).transpose(1, 0, 2).reshape(128, 256))
    wa = np.ascontiguousarray(A.transpose(2, 0, 1).reshape(DM, 128))
    wbf = np.ascontiguousarray(B.transpose(0, 2, 1).reshape(128, DM))

    ddt = np.empty((NH, 2, 128, 256), np.float32)
    for h in range(NH):
        for mc in range(2):
            ddt[h, mc] = dd64[h][:, mc * 128:(mc + 1) * 128].T.astype(np.float32)
    kdt = np.empty((128, 64), np.float32)
    qdt = np.empty((128, 64), np.float32)
    for h in range(NH):
        for mc in range(2):
            kdt[:, h * 2 + mc] = kd64[h, mc * 128:(mc + 1) * 128]
            qdt[:, h * 2 + mc] = qd64[h, mc * 128:(mc + 1) * 128]
    idm = np.eye(128, dtype=np.float32)

    shared = dict(wqk=wqk, wv=wv, wg=wg, wo=wo, wb=wb, wr=wr, wa=wa, wbf=wbf,
                  nw=norm_w, bbv=b_base, ddt=ddt, kdt=kdt, qdt=qdt, idm=idm)
    in_maps = []
    for c in range(8):
        b, q = c // 4, c % 4
        cwm = np.zeros((NH, 4), np.float32)
        for r in range(q):
            cwm[:, r] = np.exp(-256.0 * slope * 4 * (q - 1 - r)).astype(np.float32)
        m = dict(shared)
        m["x"] = np.ascontiguousarray(x[b, q * TOK:(q + 1) * TOK, :])
        m["cw"] = cwm
        in_maps.append(m)

    nc = build_nc()
    res = run_bass_kernel_spmd(nc, in_maps, core_ids=list(range(8)))
    out = np.empty((2, 4096, DM), np.float32)
    for c in range(8):
        b, q = c // 4, c % 4
        out[b, q * TOK:(q + 1) * TOK, :] = res.results[c]["out"]
    return out


if __name__ == "__main__":
    import reference as _r
    inp = _r.setup_inputs()
    o = kernel(**{k: np.asarray(v) for k, v in inp.items()})
    ref = np.asarray(_r.reference(**inp))
    err = np.mean((o - ref) ** 2) / np.mean(ref ** 2)
    print("resid var ratio", err)
